# version 02
# baseline (speedup 1.0000x reference)
import math
from contextlib import ExitStack

import numpy as np
import ml_dtypes
import concourse.bass as bass
import concourse.mybir as mybir
from concourse.bass_utils import run_bass_kernel_spmd

F32 = mybir.dt.float32
BF16 = mybir.dt.bfloat16
AF = mybir.ActivationFunctionType
ALU = mybir.AluOpType
AX = mybir.AxisListType
BF = ml_dtypes.bfloat16

P = 128
NCORES = 8
D = 1024
NCH = D // P
EPS = 1e-5


ENGS = ("sync", "vector", "scalar", "tensor", "gpsimd")


class PB:
    def __init__(self, nc):
        self.nc = nc
        self.ctx = ExitStack()
        self.sem = {e: self.ctx.enter_context(nc.semaphore(f"pb_{e}")) for e in ENGS}
        self.count = {e: 0 for e in ENGS}
        self.writer = {}
        self.readers = {}
        self.knows = {e: {} for e in ENGS}
        self.hist = {e: [(0, {})] for e in ENGS}
        self.ops = {e: [] for e in ENGS}
        self._mem = {}

    def sbuf(self, name, shape, dtype):
        if name not in self._mem:
            self._mem[name] = self.ctx.enter_context(
                self.nc.sbuf_tensor(name, shape, dtype))
        return self._mem[name]

    def psum(self, name, shape, dtype=F32):
        if name not in self._mem:
            self._mem[name] = self.ctx.enter_context(
                self.nc.psum_tensor(name, shape, dtype))
        return self._mem[name]

    def dram(self, name, shape, dtype=F32, kind="Internal"):
        key = "dram_" + name
        if key not in self._mem:
            self._mem[key] = self.nc.dram_tensor(name, shape, dtype, kind=kind).ap()
        return self._mem[key]

    def _producer_knowledge(self, pe, pv):
        h = self.hist[pe]
        lo, hi, best = 0, len(h) - 1, h[0][1]
        while lo <= hi:
            mid = (lo + hi) // 2
            if h[mid][0] <= pv:
                best = h[mid][1]
                lo = mid + 1
            else:
                hi = mid - 1
        return best

    def _gather_need(self, reads, writes):
        need = {}

        def _need(pe, pv):
            if pv > 0 and pv > need.get(pe, 0):
                need[pe] = pv

        for tok in reads:
            w = self.writer.get(tok)
            if w is not None:
                _need(*w)
        for tok in writes:
            w = self.writer.get(tok)
            if w is not None:
                _need(*w)
            for re_, rv in self.readers.get(tok, {}).items():
                _need(re_, rv)
        return need

    def _resolve_waits(self, eng, need):
        k = self.knows[eng]
        waits = []
        for pe, pv in sorted(need.items()):
            if k.get(pe, 0) >= pv:
                continue
            waits.append((pe, pv))
            pk = self._producer_knowledge(pe, pv)
            for e2, v2 in pk.items():
                if v2 > k.get(e2, 0):
                    k[e2] = v2
            k[pe] = pv
        return waits

    def op(self, eng, fn, reads=(), writes=()):
        need = self._gather_need(reads, writes)
        waits = self._resolve_waits(eng, need)
        self.count[eng] += 1
        myv = self.count[eng]
        sem = self.sem

        def closure(handle, waits=tuple(waits), fn=fn, mysem=sem[eng]):
            for pe, pv in waits:
                handle.wait_ge(sem[pe], pv)
            inst = fn(handle)
            inst.then_inc(mysem, 1)

        self.ops[eng].append(closure)
        if not hasattr(self, "dbg"):
            self.dbg = []
        self.dbg.append((eng, myv, tuple(waits), tuple(reads), tuple(writes)))
        if waits:
            self.hist[eng].append((myv, dict(self.knows[eng])))
        for tok in writes:
            self.writer[tok] = (eng, myv)
            self.readers[tok] = {}
        for tok in reads:
            self.readers.setdefault(tok, {})[eng] = myv
        return myv

    def _stream(self, stream):
        if stream not in self.sem:
            self.sem[stream] = self.ctx.enter_context(
                self.nc.semaphore(f"pb_{stream}"[:28]))
            self.count[stream] = 0
            self.hist[stream] = [(0, {})]
            self.stream_queue = getattr(self, "stream_queue", {})
        return self.sem[stream]

    def wait_all(self, eng, toks):
        need = {}
        for tok in toks:
            w = self.writer.get(tok)
            if w is not None and w[1] > need.get(w[0], 0):
                need[w[0]] = w[1]
        waits = self._resolve_waits(eng, need)
        if not waits:
            return
        sem = self.sem

        def closure(handle, todo=tuple(waits)):
            for pe, pv in todo:
                handle.wait_ge(sem[pe], pv)

        self.ops[eng].append(closure)

    def finalize(self):
        nc = self.nc
        for stream, q in getattr(self, "stream_queue", {}).items():
            fin = self.count[stream]
            k = self.knows[q]
            if k.get(stream, 0) >= fin:
                continue
            sem = self.sem

            def closure(handle, stream=stream, fin=fin):
                handle.wait_ge(sem[stream], fin)

            self.ops[q].append(closure)
            k[stream] = fin
        with nc.Block() as block:
            for eng in ENGS:
                ops = self.ops[eng]
                if not ops:
                    continue
                getattr(block, eng)(lambda h, ops=ops: [f(h) for f in ops])
        self.ctx.close()

    def dma(self, out, in_, reads=(), writes=(), queue="sync", stream=None,
            writes_final=()):
        if stream is None:
            stream = "st_" + (writes[0] if writes else queue)
        self._stream(stream)
        need = self._gather_need(reads, writes)
        waits = sorted(need.items())
        k = self.knows[queue]
        for pe, pv in waits:
            pk = self._producer_knowledge(pe, pv)
            for e2, v2 in pk.items():
                if v2 > k.get(e2, 0):
                    k[e2] = v2
            if pv > k.get(pe, 0):
                k[pe] = pv
        self.count[stream] += 16
        myv = self.count[stream]
        sem = self.sem

        def closure(handle, waits=tuple(waits), mysem=sem[stream]):
            for pe, pv in waits:
                handle.wait_ge(sem[pe], pv)
            handle.dma_start(out=out, in_=in_).then_inc(mysem, 16)

        self.ops[queue].append(closure)
        if not hasattr(self, "dbg"):
            self.dbg = []
        self.dbg.append(("dma:" + queue + ">" + stream, myv, tuple(waits), tuple(reads), tuple(writes)))
        self.stream_queue[stream] = queue
        self.hist[stream].append((myv, dict(self.knows[queue])))
        for tok in writes:
            self.writer[tok] = (stream, myv)
            self.readers[tok] = {}
        for tok in writes_final:
            self.writer[tok] = (stream, myv)
            self.readers[tok] = {}
        for tok in reads:
            self.readers.setdefault(tok, {})[stream] = myv

    def constvec(self, value, dtype=F32):
        nm = f"cv_{dtype}_{value}".replace(".", "_").replace("-", "m")
        if nm not in self._mem:
            sb = self.ctx.enter_context(self.nc.sbuf_tensor(nm, [P, 1], dtype))
            self._mem[nm] = sb
            self.op("vector", lambda h: h.memset(sb[:], value), writes=(nm,))
        return self._mem[nm], nm



def emit_const_ident(pb):
    ident = pb.sbuf("ident_sb", [P, P], BF16)
    pb.dma(ident[:], pb.dram("ident", [P, P], BF16, kind="ExternalInput"),
           writes=("ident",))
    identf = pb.sbuf("identf_sb", [P, P], F32)
    pb.op("vector", lambda h: h.tensor_copy(out=identf[:], in_=ident[:]),
          reads=("ident",), writes=("identf",))
    return ident


def emit_load_wT(pb, name, dram_ap, K, N, queue="gpsimd"):
    kch = K // P
    sb = pb.sbuf(name, [P, kch * N], BF16)
    for c in range(kch):
        pb.dma(sb[:, c * N:(c + 1) * N], dram_ap[c * P:(c + 1) * P, :],
               writes=(name,) if c == 0 else (),
               writes_final=(name,) if c == kch - 1 else (),
               queue=queue, stream="st_" + name)
    return sb, name


def emit_bcast_row(pb, name, row_dram, N, ones_sb, scale=None):
    rowb = pb.sbuf(name + "_row", [1, N], BF16)
    rsrc = row_dram.rearrange("(o n) -> o n", o=1) if len(row_dram.shape) == 1 else row_dram
    pb.dma(rowb[:], rsrc, writes=(name + "_row",), queue="gpsimd")
    out = pb.sbuf(name, [P, N], F32)
    ps = pb.psum("ps_shared512", [P, 512], F32)
    for b in range((N + 511) // 512):
        n0, n1 = b * 512, min(N, (b + 1) * 512)
        pb.op("tensor", lambda h, n0=n0, n1=n1: h.matmul(
            out=ps[:, 0:n1 - n0], lhsT=ones_sb[:], rhs=rowb[:, n0:n1],
            start=True, stop=True),
            reads=(name + "_row", "ones1", "ps_shared512"), writes=("ps_shared512w",))
        if scale is None:
            pb.op("vector", lambda h, n0=n0, n1=n1: h.tensor_copy(
                out=out[:, n0:n1], in_=ps[:, 0:n1 - n0]),
                reads=("ps_shared512w",), writes=("ps_shared512", name))
        else:
            pb.op("vector", lambda h, n0=n0, n1=n1: h.tensor_scalar_mul(
                out=out[:, n0:n1], in0=ps[:, 0:n1 - n0], scalar1=scale),
                reads=("ps_shared512w",), writes=("ps_shared512", name))
    return out, name


def emit_ln_lnT(pb, x_dram, T, name, ident, want_T=True, want_tok=False,
                width=None, colofs=0):
    ntile = T // P
    W_ = width if width is not None else T
    lnT = pb.sbuf(f"{name}_lnT", [P, NCH * W_], BF16) if want_T else None
    lntok = pb.sbuf(f"{name}_lntok", [P, ntile * D], BF16) if want_tok else None
    xt = pb.sbuf(f"{name}_xt", [P, 2 * D], F32)
    ystage = pb.sbuf(f"{name}_yst", [P, 2 * D], BF16)
    stats = pb.sbuf(f"{name}_stats", [P, 12], F32)
    mv = pb.sbuf(f"{name}_mv", [P, 4], F32)
    ptr = pb.psum(f"{name}_ptr", [P, D], BF16)
    lnTr = lnT.rearrange("p (c t) -> p c t", c=NCH) if want_T else None
    del W_

    for i in range(ntile):
        pbuf = i % 2
        xs = xt[:, pbuf * D:(pbuf + 1) * D]
        tok_x = f"{name}_xt{pbuf}"
        pb.dma(xs, x_dram[i * P:(i + 1) * P, :], reads=(tok_x,), writes=(tok_x,))
        st = f"{name}_st"
        pb.op("vector", lambda h, xs=xs: h.bn_stats(out=stats[:, 0:6], in_=xs[:, 0:512]),
              reads=(tok_x,), writes=(st + "a",))
        pb.op("vector", lambda h, xs=xs: h.bn_stats(out=stats[:, 6:12], in_=xs[:, 512:1024]),
              reads=(tok_x,), writes=(st + "b",))
        pb.op("vector", lambda h: h.bn_aggr(out=mv[:, 0:2], in_=stats[:]),
              reads=(st + "a", st + "b"), writes=(st + "mv",))
        epsv, epstok = pb.constvec(EPS)
        pb.op("scalar", lambda h: h.activation(out=mv[:, 2:3], in_=mv[:, 1:2],
                                               func=AF.Sqrt, bias=epsv[:]),
              reads=(st + "mv", epstok), writes=(st + "sd",))
        pb.op("vector", lambda h: h.reciprocal(out=mv[:, 3:4], in_=mv[:, 2:3]),
              reads=(st + "sd",), writes=(st + "is",))
        if want_tok:
            ydst = lntok[:, i * D:(i + 1) * D]
            tok_y = f"{name}_lntok_t{i}"
        else:
            ydst = ystage[:, pbuf * D:(pbuf + 1) * D]
            tok_y = f"{name}_y{pbuf}"
        pb.op("vector", lambda h, xs=xs, ydst=ydst: h.tensor_scalar(
            out=ydst, in0=xs, scalar1=mv[:, 0:1], scalar2=mv[:, 3:4],
            op0=ALU.subtract, op1=ALU.mult),
            reads=(tok_x, st + "mv", st + "is"), writes=(tok_y,))
        if want_T:
            tok_ps = f"{name}_ptr"
            wtoks = []
            for c in range(NCH):
                wt = f"{tok_ps}w{c}"
                wtoks.append(wt)
                pb.op("tensor", lambda h, c=c, ydst=ydst: h.transpose(
                    out=ptr[:, c * P:(c + 1) * P], in_=ydst[:, c * P:(c + 1) * P],
                    identity=pb._mem["ident_sb"][:]),
                    reads=(tok_y, "ident", tok_ps), writes=(wt,))

            def evac(h, i=i):
                dst = lnTr[:, :, colofs + i * P:colofs + (i + 1) * P]
                src = ptr.rearrange("p (c q) -> p c q", c=NCH)
                return h.tensor_copy(out=dst, in_=src)
            pb.op("vector", evac, reads=tuple(wtoks),
                  writes=(tok_ps, f"{name}_lnT_t{i}"))
    if want_T:
        pb.writer[f"{name}_lnT"] = self_writer_max(
            pb, [f"{name}_lnT_t{i}" for i in range(ntile)])
    if want_tok:
        pb.writer[f"{name}_lntok"] = self_writer_max(
            pb, [f"{name}_lntok_t{i}" for i in range(ntile)])
    return lnT, lntok


def self_writer_max(pb, toks):
    best = None
    for t in toks:
        w = pb.writer.get(t)
        if w is None:
            continue
        if best is None or (w[0] == best[0] and w[1] > best[1]):
            best = w
        elif w[0] != best[0]:
            raise RuntimeError("mixed-engine summary token")
    return best



def emit_l1ffn_mr(pb):
    names = [("mr_wr", "L1_ffn_w_freq_real"), ("mr_wi", "L1_ffn_w_freq_imag"),
             ("mr_fr", "c_fr"), ("mr_fi", "c_fi"), ("mr_gr", "c_gr"), ("mr_gni", "c_gni")]
    sb = {}
    for nm, dnm in names:
        dr = pb.dram(dnm, [D, D], F32, kind="ExternalInput")
        sb[nm], _ = emit_load_wT(pb, nm, dr, D, D)
    wineg = pb.sbuf("mr_wineg", [P, NCH * D], BF16)
    pb.op("vector", lambda h: h.tensor_scalar_mul(out=wineg[:], in0=sb["mr_wi"][:], scalar1=-1.0),
          reads=("mr_wi",), writes=("mr_wineg",))
    sb["mr_wineg"] = wineg
    pt = pb.sbuf("mr_pt", [P, NCH * D], BF16)
    qt = pb.sbuf("mr_qt", [P, NCH * D], BF16)
    mr = pb.sbuf("mr_out", [P, NCH * D], BF16)
    sb["mr_pt"] = pt
    sb["mr_qt"] = qt
    psb = pb.psum("mr_ps", [P, 512], F32)

    def emit_combo(dst, dstname, pairs):
        dr_ = dst.rearrange("p (c n) -> p c n", c=NCH)
        for oc in range(NCH):
            for mb in range(D // 512):
                nmm = len(pairs) * NCH
                im = 0
                for (ln, rn) in pairs:
                    lsb = sb[ln].rearrange("p (c n) -> p c n", c=NCH)
                    rsb = sb[rn].rearrange("p (c n) -> p c n", c=NCH)
                    for kc in range(NCH):
                        im += 1
                        pb.op("tensor", lambda h, lsb=lsb, rsb=rsb, kc=kc, oc=oc, mb=mb, im=im, nmm=nmm: h.matmul(
                            out=psb[:, :], lhsT=lsb[:, kc, oc * P:(oc + 1) * P],
                            rhs=rsb[:, kc, mb * 512:(mb + 1) * 512],
                            start=(im == 1), stop=(im == nmm)),
                            reads=(ln, rn, "mr_ps"),
                            writes=("mr_ps_w",) if im == nmm else ())
                pb.op("vector", lambda h, oc=oc, mb=mb, dr_=dr_: h.tensor_copy(
                    out=dr_[:, oc, mb * 512:(mb + 1) * 512], in_=psb[:, :]),
                    reads=("mr_ps_w",),
                    writes=("mr_ps",) + ((dstname,) if (oc == NCH - 1 and mb == D // 512 - 1) else ()))

    emit_combo(pt, "mr_pt", [("mr_wr", "mr_fr"), ("mr_wineg", "mr_fi")])
    emit_combo(qt, "mr_qt", [("mr_wi", "mr_fr"), ("mr_wr", "mr_fi")])
    emit_combo(mr, "mr_out", [("mr_pt", "mr_gr"), ("mr_qt", "mr_gni")])
    return mr


def emit_l1ffn(pb, x_dram, xout_dram, T, mr, biasrep, name="sf"):
    ident = pb._mem["ident_sb"]
    mrr = mr.rearrange("p (c n) -> p c n", c=NCH)
    ntile = T // P
    ps = [pb.psum(f"{name}_ps{b}", [P, 512], F32) for b in range(2)]
    psLN = pb.psum(f"{name}_psLN", [P, D], BF16)
    lnt = pb.sbuf(f"{name}_lnt", [P, D], BF16)
    lnTt = pb.sbuf(f"{name}_lnTt", [P, D], BF16)
    xres = pb.sbuf(f"{name}_xres", [P, 2 * D], F32)
    xo = pb.sbuf(f"{name}_xo", [P, 2 * D], F32)
    for i in range(ntile):
        pbuf = i % 2
        tok_xr = f"{name}_xr{pbuf}"
        pb.dma(xres[:, pbuf * D:(pbuf + 1) * D], x_dram[i * P:(i + 1) * P, :],
               reads=(tok_xr,), writes=(tok_xr,))
        emit_ln_tile(pb, name, xres[:, pbuf * D:(pbuf + 1) * D], tok_xr,
                     lnt[:], f"{name}_lnt")
        for c in range(NCH):
            pb.op("tensor", lambda h, c=c: h.transpose(
                out=psLN[:, c * P:(c + 1) * P], in_=lnt[:, c * P:(c + 1) * P],
                identity=ident[:]),
                reads=(f"{name}_lnt", "ident", f"{name}_psLN"),
                writes=(f"{name}_psLNw{c}",))
        pb.op("vector", lambda h: h.tensor_copy(out=lnTt[:], in_=psLN[:]),
              reads=tuple(f"{name}_psLNw{c}" for c in range(NCH)),
              writes=(f"{name}_psLN", f"{name}_lnTt"))
        lnTr = lnTt.rearrange("p (c t) -> p c t", c=NCH)
        for b in range(2):
            tok_ps = f"{name}_ps{b}"
            for kc in range(NCH):
                pb.op("tensor", lambda h, kc=kc, b=b, lnTr=lnTr: h.matmul(
                    out=ps[b][:, :], lhsT=lnTr[:, kc, :],
                    rhs=mrr[:, kc, b * 512:(b + 1) * 512],
                    start=(kc == 0), stop=(kc == NCH - 1)),
                    reads=(f"{name}_lnTt", "mr_out", tok_ps),
                    writes=(tok_ps + "w",) if kc == NCH - 1 else ())
            tok_xo = f"{name}_xo{pbuf}_{b}"
            pb.op("vector", lambda h, b=b, pbuf=pbuf: h.tensor_add(
                out=xo[:, pbuf * D + b * 512: pbuf * D + (b + 1) * 512],
                in0=ps[b][:, :], in1=biasrep[:, b * 512:(b + 1) * 512]),
                reads=(tok_ps + "w", "biasrep"),
                writes=(tok_ps, tok_xo + "t", tok_xo))
            pb.op("vector", lambda h, b=b, pbuf=pbuf: h.tensor_add(
                out=xo[:, pbuf * D + b * 512: pbuf * D + (b + 1) * 512],
                in0=xo[:, pbuf * D + b * 512: pbuf * D + (b + 1) * 512],
                in1=xres[:, pbuf * D + b * 512: pbuf * D + (b + 1) * 512]),
                reads=(tok_xo + "t", tok_xr), writes=(tok_xo,))
        pb.dma(xout_dram[i * P:(i + 1) * P, :], xo[:, pbuf * D:(pbuf + 1) * D],
               reads=(f"{name}_xo{pbuf}_0", f"{name}_xo{pbuf}_1"),
               writes=(f"{name}_stout{pbuf}",))
    pb.wait_all("sync", [f"{name}_stout0", f"{name}_stout1"])



def emit_ln_tile(pb, pre, xs, tok_x, ydst, tok_y):
    stats = pb.sbuf(f"{pre}_stats", [P, 12], F32)
    mv = pb.sbuf(f"{pre}_mv", [P, 4], F32)
    st = f"{pre}_st"
    pb.op("vector", lambda h: h.bn_stats(out=stats[:, 0:6], in_=xs[:, 0:512]),
          reads=(tok_x,), writes=(st + "a",))
    pb.op("vector", lambda h: h.bn_stats(out=stats[:, 6:12], in_=xs[:, 512:1024]),
          reads=(tok_x,), writes=(st + "b",))
    pb.op("vector", lambda h: h.bn_aggr(out=mv[:, 0:2], in_=stats[:]),
          reads=(st + "a", st + "b"), writes=(st + "mv",))
    epsv, epstok = pb.constvec(EPS)
    pb.op("scalar", lambda h: h.activation(out=mv[:, 2:3], in_=mv[:, 1:2],
                                           func=AF.Sqrt, bias=epsv[:]),
          reads=(st + "mv", epstok), writes=(st + "sd",))
    pb.op("vector", lambda h: h.reciprocal(out=mv[:, 3:4], in_=mv[:, 2:3]),
          reads=(st + "sd",), writes=(st + "is",))
    pb.op("vector", lambda h: h.tensor_scalar(
        out=ydst, in0=xs, scalar1=mv[:, 0:1], scalar2=mv[:, 3:4],
        op0=ALU.subtract, op1=ALU.mult),
        reads=(tok_x, st + "mv", st + "is"), writes=(tok_y,))


def emit_attn_a(pb, x_dram, xout_dram, T, name="aa"):
    import os
    STAGE = int(os.environ.get("AA_STAGE", "9999"))
    ident = pb._mem["ident_sb"]
    H, DH = 16, 64
    ntile = T // P
    wat, _ = emit_load_wT(pb, "aa_wat", pb.dram("L0_attn_a_w_attn_T", [D, 3 * D], F32,
                                                kind="ExternalInput"), D, 3 * D)
    wpr, _ = emit_load_wT(pb, "aa_wpr", pb.dram("L0_attn_a_w_proj_T", [D, D], F32,
                                                kind="ExternalInput"), D, D)
    watr = wat.rearrange("p (c n) -> p c n", c=NCH)
    wprr = wpr.rearrange("p (c n) -> p c n", c=NCH)
    ctab = pb.sbuf("aa_ctab", [P, 512], F32)
    stab = pb.sbuf("aa_stab", [P, 512], F32)
    mtab = pb.sbuf("aa_mtab", [P, 8 * P], F32)
    pb.dma(ctab[:], pb.dram("c_ropec", [P, 512], F32, kind="ExternalInput"), writes=("aa_ctab",))
    pb.dma(stab[:], pb.dram("c_ropes", [P, 512], F32, kind="ExternalInput"), writes=("aa_stab",))
    pb.dma(mtab[:], pb.dram("c_chmask", [P, 8 * P], F32, kind="ExternalInput"), writes=("aa_mtab",))

    xs2 = pb.sbuf("aa_xs", [P, 2 * D], F32)
    lnt = pb.sbuf("aa_lnt", [P, D], BF16)
    lnTt = pb.sbuf("aa_lnTt", [P, D], BF16)
    qk_tok = pb.sbuf("aa_qktok", [P, 2 * D], BF16)
    v_tok = pb.sbuf("aa_vtok", [P, D], BF16)
    qkT = pb.sbuf("aa_qkT", [P, 2 * D], BF16)
    rtmp = pb.sbuf("aa_rtmp", [P, 512], F32)
    rtmp2 = pb.sbuf("aa_rtmp2", [P, 512], F32)
    smask = pb.sbuf("aa_smask", [P, 1024], F32)
    sexp = pb.sbuf("aa_sexp", [P, 1024], F32)
    pnorm = pb.sbuf("aa_pnorm", [P, 1024], BF16)
    ptsb = pb.sbuf("aa_ptsb", [P, 1024], BF16)
    red = pb.sbuf("aa_red", [P, 24], F32)
    yT = pb.sbuf("aa_yT", [P, D], BF16)
    xo = pb.sbuf("aa_xo", [P, 2 * D], F32)
    psA = pb.psum("aa_psA", [P, 512], F32)
    psB = pb.psum("aa_psB", [P, 512], F32)
    psLN = pb.psum("aa_psLN", [P, D], BF16)
    psS = pb.psum("aa_psS", [P, 1024], F32)
    psPT = pb.psum("aa_psPT", [P, 1024], BF16)
    psY = pb.psum("aa_psY", [P, 512], F32)

    for i in range(ntile):
        pbuf = i % 2
        xs = xs2[:, pbuf * D:(pbuf + 1) * D]
        tok_x = f"aa_x{pbuf}"
        pb.dma(xs, x_dram[i * P:(i + 1) * P, :], reads=(tok_x,), writes=(tok_x,))
        emit_ln_tile(pb, "aa", xs, tok_x, lnt[:], "aa_lnt")
        for c in range(NCH):
            pb.op("tensor", lambda h, c=c: h.transpose(
                out=psLN[:, c * P:(c + 1) * P], in_=lnt[:, c * P:(c + 1) * P],
                identity=ident[:]),
                reads=("aa_lnt", "ident", "aa_psLN"), writes=(f"aa_psLNw{c}",))
        pb.op("vector", lambda h: h.tensor_copy(out=lnTt[:], in_=psLN[:]),
              reads=tuple(f"aa_psLNw{c}" for c in range(NCH)),
              writes=("aa_psLN", "aa_lnTt"))
        lnTr = lnTt.rearrange("p (c t) -> p c t", c=NCH)
        if STAGE <= 1:
            pb.op("vector", lambda h, pbuf=pbuf: h.tensor_copy(
                out=xo[:, pbuf * D:(pbuf + 1) * D], in_=lnTt[:]),
                reads=("aa_lnTt",), writes=(f"aa_xo{pbuf}_0", f"aa_xo{pbuf}_1", f"aa_xo{pbuf}"))
            pb.dma(xout_dram[i * P:(i + 1) * P, :], xo[:, pbuf * D:(pbuf + 1) * D],
                   reads=(f"aa_xo{pbuf}_0",), writes=(f"aa_stout{pbuf}",))
            continue

        for blk in range(6):
            ps = psA if blk % 2 == 0 else psB
            pstok = "aa_psA" if blk % 2 == 0 else "aa_psB"
            for kc in range(NCH):
                pb.op("tensor", lambda h, kc=kc, blk=blk, ps=ps: h.matmul(
                    out=ps[:], lhsT=lnTr[:, kc, :], rhs=watr[:, kc, blk * 512:(blk + 1) * 512],
                    start=(kc == 0), stop=(kc == NCH - 1)),
                    reads=("aa_lnTt", "aa_wat", pstok),
                    writes=(pstok + "w",) if kc == NCH - 1 else ())
            if blk < 4:
                dst = qk_tok[:, blk * 512:(blk + 1) * 512]
                swap = ps.rearrange("p (hh half d) -> p hh half d", hh=8, half=2)[:, :, ::-1, :]
                pb.op("vector", lambda h, ps=ps: h.tensor_mul(
                    out=rtmp[:], in0=ps[:], in1=ctab[:]),
                    reads=(pstok + "w", "aa_ctab"), writes=("aa_rt1",))
                pb.op("vector", lambda h, swap=swap: h.tensor_tensor(
                    out=rtmp2[:], in0=swap, in1=stab.rearrange(
                        "p (hh half d) -> p hh half d", hh=8, half=2),
                    op=ALU.mult),
                    reads=(pstok + "w", "aa_stab"), writes=("aa_rt2",))
                pb.op("vector", lambda h, dst=dst: h.tensor_add(
                    out=dst, in0=rtmp[:], in1=rtmp2[:]),
                    reads=("aa_rt1", "aa_rt2"), writes=(pstok, f"aa_qk{blk}",))
            else:
                dst = v_tok[:, (blk - 4) * 512:(blk - 3) * 512]
                pb.op("vector", lambda h, ps=ps, dst=dst: h.tensor_copy(out=dst, in_=ps[:]),
                      reads=(pstok + "w",), writes=(pstok, f"aa_v{blk - 4}",))
        if STAGE <= 2:
            pb.op("vector", lambda h, pbuf=pbuf: h.tensor_copy(
                out=xo[:, pbuf * D:(pbuf + 1) * D], in_=qk_tok[:, 0:D]),
                reads=("aa_qk0", "aa_qk1"), writes=(f"aa_xo{pbuf}_0", f"aa_xo{pbuf}"))
            pb.dma(xout_dram[i * P:(i + 1) * P, :], xo[:, pbuf * D:(pbuf + 1) * D],
                   reads=(f"aa_xo{pbuf}_0",), writes=(f"aa_stout{pbuf}",))
            continue
        for c in range(2 * NCH):
            pb.op("tensor", lambda h, c=c: h.transpose(
                out=psPT[:, (c % NCH) * P:(c % NCH + 1) * P],
                in_=qk_tok[:, c * P:(c + 1) * P], identity=ident[:]),
                reads=(f"aa_qk{c // 4}", "ident", "aa_psPT"),
                writes=(f"aa_psPTw{c % NCH}",))
            if c % NCH == NCH - 1:
                half = c // NCH
                pb.op("vector", lambda h, half=half: h.tensor_copy(
                    out=qkT[:, half * D:(half + 1) * D], in_=psPT[:]),
                    reads=tuple(f"aa_psPTw{cc}" for cc in range(NCH)),
                    writes=("aa_psPT", f"aa_qkT{half}"))
        qTr = qkT.rearrange("p (half c t) -> p half c t", half=2, c=NCH)
        if STAGE <= 21:
            pb.op("vector", lambda h, pbuf=pbuf: h.tensor_copy(
                out=xo[:, pbuf * D:(pbuf + 1) * D], in_=qkT[:, 0:D]),
                reads=("aa_qkT0", "aa_qkT1"), writes=(f"aa_xo{pbuf}_0", f"aa_xo{pbuf}"))
            pb.dma(xout_dram[i * P:(i + 1) * P, :], xo[:, pbuf * D:(pbuf + 1) * D],
                   reads=(f"aa_xo{pbuf}_0",), writes=(f"aa_stout{pbuf}",))
            continue

        import os as _os
        SLOT = [0,1,2,3,4,5,6,7] if _os.environ.get('SLOTID') else [0, 2, 4, 6, 1, 3, 5, 7]
        for g in range(2):
            for s_ in range(8):
                hh = g * 8 + SLOT[s_]
                c, r0 = hh // 2, (hh % 2) * 64
                pb.op("tensor", lambda h, s_=s_, c=c, r0=r0: h.matmul(
                    out=psS[:, s_ * P:(s_ + 1) * P],
                    lhsT=qTr[r0:r0 + 64, 0, c, :], rhs=qTr[r0:r0 + 64, 1, c, :],
                    start=True, stop=True),
                    reads=("aa_qkT0", "aa_qkT1", "aa_psS"),
                    writes=(f"aa_psSw{s_}",))
            if STAGE <= 215:
                pb.op("vector", lambda h, pbuf=pbuf, g=g: h.tensor_copy(
                    out=xo[:, pbuf * D:(pbuf + 1) * D], in_=psS[:]),
                    reads=tuple(f"aa_psSw{h_}" for h_ in range(8)),
                    writes=("aa_psS", f"aa_xo{pbuf}_0", f"aa_xo{pbuf}"))
                if g == 1:
                    pb.dma(xout_dram[i * P:(i + 1) * P, :], xo[:, pbuf * D:(pbuf + 1) * D],
                           reads=(f"aa_xo{pbuf}_0",), writes=(f"aa_stout{pbuf}",))
                continue
            pb.op("vector", lambda h: h.tensor_tensor(
                out=smask[:], in0=psS[:], in1=mtab[:], op=ALU.add),
                reads=tuple(f"aa_psSw{h_}" for h_ in range(8)) + ("aa_mtab",),
                writes=("aa_psS", "aa_smask"))
            if STAGE <= 216:
                pb.op("vector", lambda h, pbuf=pbuf: h.tensor_copy(
                    out=xo[:, pbuf * D:(pbuf + 1) * D], in_=smask[:]),
                    reads=("aa_smask",), writes=(f"aa_xo{pbuf}_0", f"aa_xo{pbuf}"))
                if g == 1:
                    pb.dma(xout_dram[i * P:(i + 1) * P, :], xo[:, pbuf * D:(pbuf + 1) * D],
                           reads=(f"aa_xo{pbuf}_0",), writes=(f"aa_stout{pbuf}",))
                continue
            pb.op("vector", lambda h: h.reduce_max(
                out=red[:, 0:8], in_=smask.rearrange("p (hh k) -> p hh k", hh=8),
                axis=AX.X),
                reads=("aa_smask",), writes=("aa_rmax",))
            pb.op("vector", lambda h: h.tensor_tensor(
                out=smask[:], in0=smask[:],
                in1=red[:, 0:8].rearrange("p (hh o) -> p hh o", o=1).broadcast_to([P, 8, P]),
                op=ALU.subtract),
                reads=("aa_smask", "aa_rmax"), writes=("aa_smask",))
            if STAGE <= 217:
                pb.op("vector", lambda h, pbuf=pbuf: h.tensor_copy(
                    out=xo[:, pbuf * D:(pbuf + 1) * D], in_=smask[:]),
                    reads=("aa_smask",), writes=(f"aa_xo{pbuf}_0", f"aa_xo{pbuf}"))
                if g == 1:
                    pb.dma(xout_dram[i * P:(i + 1) * P, :], xo[:, pbuf * D:(pbuf + 1) * D],
                           reads=(f"aa_xo{pbuf}_0",), writes=(f"aa_stout{pbuf}",))
                continue
            pb.op("scalar", lambda h: h.activation(
                out=sexp[:], in_=smask[:], func=AF.Exp, scale=0.125),
                reads=("aa_smask",), writes=("aa_sexp",))
            pb.op("vector", lambda h: h.reduce_sum(
                out=red[:, 8:16], in_=sexp.rearrange("p (hh k) -> p hh k", hh=8),
                axis=AX.X),
                reads=("aa_sexp",), writes=("aa_rsum",))
            pb.op("vector", lambda h: h.reciprocal(out=red[:, 16:24], in_=red[:, 8:16]),
                  reads=("aa_rsum",), writes=("aa_rrec",))
            pb.op("vector", lambda h: h.tensor_tensor(
                out=pnorm[:], in0=sexp[:],
                in1=red[:, 16:24].rearrange("p (hh o) -> p hh o", o=1).broadcast_to([P, 8, P]),
                op=ALU.mult),
                reads=("aa_sexp", "aa_rrec"), writes=("aa_pnorm",))
            if STAGE <= 22:
                pb.op("vector", lambda h, pbuf=pbuf, g=g: h.tensor_copy(
                    out=xo[:, pbuf * D:(pbuf + 1) * D], in_=pnorm[:]),
                    reads=("aa_pnorm",), writes=(f"aa_xo{pbuf}_0", f"aa_xo{pbuf}"))
                if g == 1:
                    pb.dma(xout_dram[i * P:(i + 1) * P, :], xo[:, pbuf * D:(pbuf + 1) * D],
                           reads=(f"aa_xo{pbuf}_0",), writes=(f"aa_stout{pbuf}",))
                continue
            for h_ in range(8):
                pb.op("tensor", lambda h, h_=h_: h.transpose(
                    out=psPT[:, h_ * P:(h_ + 1) * P], in_=pnorm[:, h_ * P:(h_ + 1) * P],
                    identity=ident[:]),
                    reads=("aa_pnorm", "ident", "aa_psPT"), writes=(f"aa_psPTw{h_}",))
            pb.op("vector", lambda h: h.tensor_copy(out=ptsb[:], in_=psPT[:]),
                  reads=tuple(f"aa_psPTw{h_}" for h_ in range(8)),
                  writes=("aa_psPT", "aa_ptsb"))
            if STAGE <= 23:
                pb.op("vector", lambda h, pbuf=pbuf, g=g: h.tensor_copy(
                    out=xo[:, pbuf * D:(pbuf + 1) * D], in_=ptsb[:, 0:D]),
                    reads=("aa_ptsb",), writes=(f"aa_xo{pbuf}_0", f"aa_xo{pbuf}"))
                if g == 1:
                    pb.dma(xout_dram[i * P:(i + 1) * P, :], xo[:, pbuf * D:(pbuf + 1) * D],
                           reads=(f"aa_xo{pbuf}_0",), writes=(f"aa_stout{pbuf}",))
                continue
            for s_ in range(8):
                hh = g * 8 + SLOT[s_]
                c, r0 = hh // 2, (hh % 2) * 64
                pb.op("tensor", lambda h, s_=s_, hh=hh, c=c, r0=r0, g=g: h.matmul(
                    out=psY[r0:r0 + 64, c * P - g * 512:(c + 1) * P - g * 512],
                    lhsT=v_tok[:, hh * 64:(hh + 1) * 64], rhs=ptsb[:, s_ * P:(s_ + 1) * P],
                    start=True, stop=True),
                    reads=(f"aa_v{hh // 8}", "aa_ptsb", "aa_psY"),
                    writes=(f"aa_psYw{g}_{s_}",))
            pb.op("vector", lambda h, g=g: h.tensor_copy(
                out=yT[:, g * 512:(g + 1) * 512], in_=psY[:, 0:512]),
                reads=tuple(f"aa_psYw{g}_{h_}" for h_ in range(8)),
                writes=("aa_psY", f"aa_yT{g}"))
        if STAGE <= 3:
            pb.op("vector", lambda h, pbuf=pbuf: h.tensor_copy(
                out=xo[:, pbuf * D:(pbuf + 1) * D], in_=yT[:]),
                reads=("aa_yT0", "aa_yT1"), writes=(f"aa_xo{pbuf}_0", f"aa_xo{pbuf}"))
            pb.dma(xout_dram[i * P:(i + 1) * P, :], xo[:, pbuf * D:(pbuf + 1) * D],
                   reads=(f"aa_xo{pbuf}_0",), writes=(f"aa_stout{pbuf}",))
            continue
        yTr = yT.rearrange("p (c t) -> p c t", c=NCH)
        tok_xo = f"aa_xo{pbuf}"
        for b in range(2):
            ps = psA if b == 0 else psB
            pstok = "aa_psA" if b == 0 else "aa_psB"
            for kc in range(NCH):
                pb.op("tensor", lambda h, kc=kc, b=b, ps=ps: h.matmul(
                    out=ps[:], lhsT=yTr[:, kc, :], rhs=wprr[:, kc, b * 512:(b + 1) * 512],
                    start=(kc == 0), stop=(kc == NCH - 1)),
                    reads=("aa_yT0", "aa_yT1", "aa_wpr", pstok),
                    writes=(pstok + "w",) if kc == NCH - 1 else ())
            pb.op("vector", lambda h, b=b, ps=ps, pbuf=pbuf, xs=xs: h.tensor_add(
                out=xo[:, pbuf * D + b * 512:pbuf * D + (b + 1) * 512],
                in0=ps[:], in1=xs[:, b * 512:(b + 1) * 512]),
                reads=(pstok + "w", tok_x), writes=(pstok, tok_xo + f"_{b}", tok_xo))
        pb.dma(xout_dram[i * P:(i + 1) * P, :], xo[:, pbuf * D:(pbuf + 1) * D],
               reads=(tok_xo + "_0", tok_xo + "_1"), writes=(f"aa_stout{pbuf}",))
    pb.wait_all("sync", ["aa_stout0", "aa_stout1"])


def host_attn_a_consts():
    cs, sn = np.zeros((P, 512), np.float32), np.zeros((P, 512), np.float32)
    inv = 1.0 / 10000.0 ** (np.arange(0, 64, 2) / 64.0)
    t = np.arange(64)
    fr = np.outer(t, inv)
    cosv, sinv = np.cos(fr), np.sin(fr)
    for p in range(P):
        for hh in range(8):
            cs[p, hh * 64:hh * 64 + 32] = cosv[p % 64]
            cs[p, hh * 64 + 32:hh * 64 + 64] = cosv[p % 64]
            sn[p, hh * 64:hh * 64 + 32] = sinv[p % 64]
            sn[p, hh * 64 + 32:hh * 64 + 64] = -sinv[p % 64]
    mask = np.full((P, P), -1e9, np.float32)
    mask[0:64, 0:64] = 0.0
    mask[64:128, 64:128] = 0.0
    return cs, sn, np.tile(mask, (1, 8))



def emit_moe_mask(pb, x_dram, T, name, lnname, partial_dram, mask_dram, ident):
    ntile = T // P
    lnT, _ = emit_ln_lnT(pb, x_dram, T, lnname, ident, want_T=True)
    rw, _ = emit_load_wT(pb, f"{name}_rw", pb.dram("L0_ffn_router_w", [D, 8], F32,
                                                   kind="ExternalInput"), D, 8)
    rwr = rw.rearrange("p (c n) -> p c n", c=NCH)
    ones1 = pb._mem["ones1_sb"]
    rb_rep, _ = emit_bcast_row(pb, f"{name}_rb", pb.dram("L0_ffn_router_b", [8], F32,
                                                         kind="ExternalInput"), 8, ones1)
    mask_sb = pb.sbuf(f"{name}_mask", [P, ntile * 8], F32)
    lg = pb.sbuf(f"{name}_lg", [P, 8], F32)
    pr = pb.sbuf(f"{name}_pr", [P, 8], F32)
    pr2 = pb.sbuf(f"{name}_pr2", [P, 8], F32)
    is1 = pb.sbuf(f"{name}_is1", [P, 8], F32)
    is2 = pb.sbuf(f"{name}_is2", [P, 8], F32)
    t1s = pb.sbuf(f"{name}_t1s", [P, 16], F32)
    red = pb.sbuf(f"{name}_red", [P, 8], F32)
    onecol = pb.sbuf(f"{name}_onecol", [P, 1], F32)
    psr = pb.psum(f"{name}_psr", [P, 8], F32)
    psg = pb.psum(f"{name}_psg", [1, 16], F32)
    lnTr = lnT.rearrange("p (c t) -> p c t", c=NCH)
    pb.op("vector", lambda h: h.memset(onecol[:], 1.0), writes=(f"{name}_onecol",))

    for i in range(ntile):
        for kc in range(NCH):
            pb.op("tensor", lambda h, kc=kc, i=i: h.matmul(
                out=psr[:], lhsT=lnTr[:, kc, i * P:(i + 1) * P], rhs=rwr[:, kc, :],
                start=(kc == 0), stop=(kc == NCH - 1)),
                reads=(f"{lnname}_lnT_t{i}", f"{name}_rw", f"{name}_psr"),
                writes=(f"{name}_psrw",) if kc == NCH - 1 else ())
        pb.op("vector", lambda h: h.tensor_add(out=lg[:], in0=psr[:], in1=rb_rep[:, 0:8]),
              reads=(f"{name}_psrw", f"{name}_rb"), writes=(f"{name}_psr", f"{name}_lg"))
        pb.op("vector", lambda h: h.reduce_max(out=red[:, 0:1], in_=lg[:], axis=AX.X),
              reads=(f"{name}_lg",), writes=(f"{name}_rm",))
        pb.op("vector", lambda h: h.tensor_scalar_mul(out=red[:, 1:2], in0=red[:, 0:1], scalar1=-1.0),
              reads=(f"{name}_rm",), writes=(f"{name}_rmn",))
        pb.op("scalar", lambda h: h.activation(out=pr[:], in_=lg[:], func=AF.Exp,
                                               bias=red[:, 1:2]),
              reads=(f"{name}_lg", f"{name}_rmn"), writes=(f"{name}_pre",))
        pb.op("vector", lambda h: h.reduce_sum(out=red[:, 2:3], in_=pr[:], axis=AX.X),
              reads=(f"{name}_pre",), writes=(f"{name}_rs",))
        pb.op("vector", lambda h: h.reciprocal(out=red[:, 3:4], in_=red[:, 2:3]),
              reads=(f"{name}_rs",), writes=(f"{name}_rsi",))
        pb.op("vector", lambda h: h.tensor_scalar_mul(out=pr[:], in0=pr[:], scalar1=red[:, 3:4]),
              reads=(f"{name}_pre", f"{name}_rsi"), writes=(f"{name}_pr",))
        pb.op("vector", lambda h: h.reduce_max(out=red[:, 4:5], in_=pr[:], axis=AX.X),
              reads=(f"{name}_pr",), writes=(f"{name}_m1",))
        pb.op("vector", lambda h: h.tensor_single_scalar(out=is1[:], in_=pr[:],
                                                         scalar=red[:, 4:5], op=ALU.is_equal),
              reads=(f"{name}_pr", f"{name}_m1"), writes=(f"{name}_is1",))
        pb.op("vector", lambda h: h.tensor_scalar(out=pr2[:], in0=is1[:], scalar1=-1.0,
                                                  scalar2=1.0, op0=ALU.mult, op1=ALU.add),
              reads=(f"{name}_is1",), writes=(f"{name}_pr2a",))
        pb.op("vector", lambda h: h.tensor_mul(out=pr2[:], in0=pr2[:], in1=pr[:]),
              reads=(f"{name}_pr2a", f"{name}_pr"), writes=(f"{name}_pr2",))
        pb.op("vector", lambda h: h.reduce_max(out=red[:, 5:6], in_=pr2[:], axis=AX.X),
              reads=(f"{name}_pr2",), writes=(f"{name}_m2",))
        pb.op("vector", lambda h: h.tensor_single_scalar(out=is2[:], in_=pr2[:],
                                                         scalar=red[:, 5:6], op=ALU.is_equal),
              reads=(f"{name}_pr2", f"{name}_m2"), writes=(f"{name}_is2",))
        pb.op("vector", lambda h: h.tensor_add(out=red[:, 6:7], in0=red[:, 4:5], in1=red[:, 5:6]),
              reads=(f"{name}_m1", f"{name}_m2"), writes=(f"{name}_md",))
        pb.op("vector", lambda h: h.reciprocal(out=red[:, 7:8], in0=red[:, 6:7]) if False else h.reciprocal(out=red[:, 7:8], in_=red[:, 6:7]),
              reads=(f"{name}_md",), writes=(f"{name}_mdi",))
        pb.op("vector", lambda h: h.tensor_scalar(out=t1s[:, 0:8], in0=is1[:],
                                                  scalar1=red[:, 4:5], scalar2=red[:, 7:8],
                                                  op0=ALU.mult, op1=ALU.mult),
              reads=(f"{name}_is1", f"{name}_m1", f"{name}_mdi"), writes=(f"{name}_t1a",))
        pb.op("vector", lambda h: h.tensor_scalar(out=t1s[:, 8:16], in0=is2[:],
                                                  scalar1=red[:, 5:6], scalar2=red[:, 7:8],
                                                  op0=ALU.mult, op1=ALU.mult),
              reads=(f"{name}_is2", f"{name}_m2", f"{name}_mdi"), writes=(f"{name}_t1b2",))
        pb.op("vector", lambda h, i=i: h.tensor_add(
            out=mask_sb[:, i * 8:(i + 1) * 8], in0=t1s[:, 0:8], in1=t1s[:, 8:16]),
            reads=(f"{name}_t1a", f"{name}_t1b2"), writes=(f"{name}_mask_t{i}",))
        pb.op("tensor", lambda h, i=i: h.matmul(
            out=psg[:], lhsT=onecol[:], rhs=t1s[:],
            start=(i == 0), stop=(i == ntile - 1)),
            reads=(f"{name}_t1a", f"{name}_t1b2", f"{name}_onecol", f"{name}_psg"),
            writes=(f"{name}_psgw",) if i == ntile - 1 else ())
    gout = pb.sbuf(f"{name}_gout", [1, 16], F32)
    pb.op("vector", lambda h: h.tensor_copy(out=gout[:], in_=psg[:]),
          reads=(f"{name}_psgw",), writes=(f"{name}_gout",))
    pb.dma(partial_dram, gout[:], reads=(f"{name}_gout",), writes=(f"{name}_gst",))
    mdst = mask_dram.rearrange("(n p) e -> p n e", p=P)
    pb.dma(mdst, mask_sb[:], reads=tuple(f"{name}_mask_t{i}" for i in range(ntile)),
           writes=(f"{name}_mst",))
    pb.wait_all("sync", [f"{name}_gst", f"{name}_mst"])


def emit_moe_experts(pb, x_dram, xout_dram, T, name, mask_dram, gtr_dram, ident):
    ntile = T // P
    ngrp = T // 1024
    ones1 = pb._mem["ones1_sb"]
    mask_sb = pb.sbuf(f"{name}_mask", [P, ntile * 8], F32)
    pb.dma(mask_sb[:], mask_dram.rearrange("(n p) e -> p n e", p=P),
           writes=(f"{name}_maskl",))
    gtr = pb.sbuf(f"{name}_gtr", [8, 16], F32)
    pb.dma(gtr[:], gtr_dram, writes=(f"{name}_gtrl",))
    gsum = pb.sbuf(f"{name}_gsum", [8, 2], F32)
    flagv = pb.sbuf(f"{name}_flag", [P, 1], F32)
    pb.dma(flagv[:], pb.dram("c_core0", [P, 1], F32, kind="ExternalInput"),
           writes=(f"{name}_flagl",))
    pb.op("vector", lambda h: h.reduce_sum(
        out=gsum[:], in_=gtr.rearrange("p (j c) -> p j c", j=2), axis=AX.X),
        reads=(f"{name}_gtrl",), writes=(f"{name}_gsum",))
    pb.op("vector", lambda h: h.tensor_scalar_mul(out=gsum[:], in0=gsum[:],
                                                  scalar1=flagv[0:8, 0:1]),
          reads=(f"{name}_gsum", f"{name}_flagl"), writes=(f"{name}_gsum",))
    pb.op("vector", lambda h: h.tensor_add(out=mask_sb[0:8, 0:2], in0=mask_sb[0:8, 0:2],
                                           in1=gsum[:]),
          reads=(f"{name}_maskl", f"{name}_gsum"), writes=(f"{name}_maskl",))
    pb.op("vector", lambda h: h.tensor_scalar_min(out=mask_sb[0:8, 0:8], in0=mask_sb[0:8, 0:8],
                                                  scalar1=5120.0),
          reads=(f"{name}_maskl",), writes=(f"{name}_mask",))
    maskT = pb.sbuf(f"{name}_maskT", [8, T], BF16)
    maskbf = pb.sbuf(f"{name}_maskbf", [P, 8], BF16)
    psmt = pb.psum(f"{name}_psmt", [8, P], BF16)
    for i in range(ntile):
        pb.op("vector", lambda h, i=i: h.tensor_copy(out=maskbf[:], in_=mask_sb[:, i * 8:(i + 1) * 8]),
              reads=(f"{name}_mask",), writes=(f"{name}_maskbf",))
        pb.op("tensor", lambda h: h.transpose(out=psmt[:], in_=maskbf[:], identity=ident[:]),
              reads=(f"{name}_maskbf", "ident", f"{name}_psmt"), writes=(f"{name}_psmtw",))
        pb.op("vector", lambda h, i=i: h.tensor_copy(
            out=maskT[:, i * P:(i + 1) * P], in_=psmt[:]),
            reads=(f"{name}_psmtw",), writes=(f"{name}_psmt", f"{name}_maskT",))
    mtdram = pb.dram(f"{name}_mtdram", [8, T], BF16, kind="Internal")
    pb.dma(mtdram, maskT[:], reads=(f"{name}_maskT",), writes=(f"{name}_mtd",),
           stream=f"st_{name}_mtd")
    mstg = pb.sbuf(f"{name}_mstg", [1, 512], BF16)
    b1raw = pb.sbuf(f"{name}_b1raw", [8, D], BF16)
    b2sb = pb.sbuf(f"{name}_b2", [8, D], BF16)
    pb.dma(b1raw[:], pb.dram("L0_ffn_b1", [8, D], F32, kind="ExternalInput"),
           writes=(f"{name}_b1raw",), queue="gpsimd")
    pb.dma(b2sb[:], pb.dram("L0_ffn_b2", [8, D], F32, kind="ExternalInput"),
           writes=(f"{name}_b2",), queue="gpsimd")
    b1T = pb.sbuf(f"{name}_b1T", [P, 8 * NCH], BF16)
    psbt = pb.psum(f"{name}_psbt", [P, 8], BF16)
    for c in range(NCH):
        pb.op("tensor", lambda h, c=c: h.transpose(
            out=psbt[:], in_=b1raw[:, c * P:(c + 1) * P], identity=ident[0:8, 0:8]),
            reads=(f"{name}_b1raw", "ident", f"{name}_psbt"), writes=(f"{name}_psbtw",))
        pb.op("vector", lambda h, c=c: h.tensor_copy(
            out=b1T[:, c * 8:(c + 1) * 8], in_=psbt[:]),
            reads=(f"{name}_psbtw",), writes=(f"{name}_psbt", f"{name}_b1T",))
    w1d = pb.dram("L0_ffn_w1", [8, D, D], F32, kind="ExternalInput")
    w2d = pb.dram("L0_ffn_w2", [8, D, D], F32, kind="ExternalInput")
    w1b = [pb.sbuf(f"{name}_w1_{j}", [P, NCH * D], BF16) for j in range(2)]
    w2b = pb.sbuf(f"{name}_w2", [P, NCH * D], BF16)
    hsb = pb.sbuf(f"{name}_h", [P, NCH * 512], BF16)
    mrep = pb.sbuf(f"{name}_mrep", [P, 512], BF16)
    hm = pb.sbuf(f"{name}_hm", [P, 512], F32)
    acc = pb.sbuf(f"{name}_acc", [P, 8 * D], F32)
    xres = pb.sbuf(f"{name}_xres", [P, 2 * D], F32)
    xo = pb.sbuf(f"{name}_xo", [P, 2 * D], F32)
    psF = [pb.psum(f"{name}_psF{j}", [P, 512], F32) for j in range(2)]
    psO = [pb.psum(f"{name}_psO{j}", [P, 512], F32) for j in range(2)]
    psM = pb.psum(f"{name}_psM", [P, 512], F32)

    def load_w1(e):
        j = e % 2
        for c in range(NCH):
            pb.dma(w1b[j][:, c * D:(c + 1) * D], w1d[e, c * P:(c + 1) * P, :],
                   writes=(f"{name}_w1_{j}",) if c == 0 else (),
                   writes_final=(f"{name}_w1_{j}",) if c == NCH - 1 else (),
                   queue="gpsimd", stream=f"st_{name}_w1_{j}")

    def load_w2(e):
        for c in range(NCH):
            pb.dma(w2b[:, c * D:(c + 1) * D], w2d[e, c * P:(c + 1) * P, :],
                   writes=(f"{name}_w2",) if c == 0 else (),
                   writes_final=(f"{name}_w2",) if c == NCH - 1 else (),
                   queue="gpsimd", stream=f"st_{name}_w2")

    for g in range(ngrp):
        xg = x_dram[g * 1024:(g + 1) * 1024, :]
        lnT, _ = emit_ln_lnT(pb, xg, 1024, f"lnq{name}", ident, want_T=True)
        lnTr = lnT.rearrange("p (c t) -> p c t", c=NCH)
        pb.op("vector", lambda h: h.memset(acc[:], 0.0),
              reads=(), writes=(f"{name}_acc",))
        for e in range(8):
            j = e % 2
            if e == 0:
                load_w1(0)
            load_w2(e)
            if e < 7:
                load_w1(e + 1)
            w1r = w1b[j].rearrange("p (c n) -> p c n", c=NCH)
            w2r = w2b.rearrange("p (c n) -> p c n", c=NCH)
            for tb in range(2):
                gt0 = g * 8 + tb * 4
                pb.dma(mstg[:], mtdram[e:e + 1, g * 1024 + tb * 512:g * 1024 + (tb + 1) * 512],
                       reads=(f"{name}_mtd", f"{name}_mstg"), writes=(f"{name}_mstg",))
                pb.op("tensor", lambda h: h.matmul(
                    out=psM[:], lhsT=ones1[:], rhs=mstg[:],
                    start=True, stop=True),
                    reads=(f"{name}_mstg", "ones1", f"{name}_psM"), writes=(f"{name}_psMw",))
                pb.op("vector", lambda h: h.tensor_copy(out=mrep[:], in_=psM[:]),
                      reads=(f"{name}_psMw",), writes=(f"{name}_psM", f"{name}_mrep"))
                for hc in range(NCH):
                    jf = hc % 2
                    for kc in range(NCH):
                        pb.op("tensor", lambda h, hc=hc, kc=kc, tb=tb, jf=jf, w1r=w1r: h.matmul(
                            out=psF[jf][:], lhsT=w1r[:, kc, hc * P:(hc + 1) * P],
                            rhs=lnTr[:, kc, tb * 512:(tb + 1) * 512],
                            start=(kc == 0), stop=(kc == NCH - 1)),
                            reads=(f"{name}_w1_{j}", f"lnq{name}_lnT", f"{name}_psF{jf}"),
                            writes=(f"{name}_psF{jf}w",) if kc == NCH - 1 else ())
                    pb.op("vector", lambda h, jf=jf: h.tensor_mul(out=hm[:], in0=psF[jf][:], in1=mrep[:]),
                          reads=(f"{name}_psF{jf}w", f"{name}_mrep"),
                          writes=(f"{name}_psF{jf}", f"{name}_hm"))
                    pb.op("scalar", lambda h, hc=hc, e=e: h.activation(
                        out=hsb[:, hc * 512:(hc + 1) * 512], in_=hm[:], func=AF.Gelu,
                        bias=b1T[:, hc * 8 + e:hc * 8 + e + 1]),
                        reads=(f"{name}_hm", f"{name}_b1T"), writes=(f"{name}_h{hc}",))
                hsr = hsb.rearrange("p (c n) -> p c n", c=NCH)
                for s in range(4):
                    i_loc = tb * 4 + s
                    i_glob = g * 8 + i_loc
                    for bk in range(2):
                        for hc in range(NCH):
                            pb.op("tensor", lambda h, hc=hc, s=s, bk=bk, w2r=w2r: h.matmul(
                                out=psO[bk][:], lhsT=hsr[:, hc, s * P:(s + 1) * P],
                                rhs=w2r[:, hc, bk * 512:(bk + 1) * 512],
                                start=(hc == 0), stop=(hc == NCH - 1)),
                                reads=tuple(f"{name}_h{hc2}" for hc2 in range(NCH)) + (
                                    f"{name}_w2", f"{name}_psO{bk}"),
                                writes=(f"{name}_psO{bk}w",) if hc == NCH - 1 else ())
                        pb.op("vector", lambda h, i_loc=i_loc, bk=bk, i_glob=i_glob, e=e: h.scalar_tensor_tensor(
                            out=acc[:, i_loc * D + bk * 512:i_loc * D + (bk + 1) * 512],
                            in0=psO[bk][:], scalar=mask_sb[:, i_glob * 8 + e:i_glob * 8 + e + 1],
                            in1=acc[:, i_loc * D + bk * 512:i_loc * D + (bk + 1) * 512],
                            op0=ALU.mult, op1=ALU.add),
                            reads=(f"{name}_psO{bk}w", f"{name}_mask", f"{name}_acc"),
                            writes=(f"{name}_psO{bk}", f"{name}_acc"))
        for i_loc in range(8):
            i_glob = g * 8 + i_loc
            pbuf = i_loc % 2
            tok_xr = f"{name}_xr{pbuf}"
            pb.dma(xres[:, pbuf * D:(pbuf + 1) * D], x_dram[i_glob * P:(i_glob + 1) * P, :],
                   reads=(tok_xr,), writes=(tok_xr,))
            for bk in range(2):
                pb.op("tensor", lambda h, i_glob=i_glob, bk=bk: h.matmul(
                    out=psO[bk][:], lhsT=maskT[:, i_glob * P:(i_glob + 1) * P],
                    rhs=b2sb[:, bk * 512:(bk + 1) * 512],
                    start=True, stop=True),
                    reads=(f"{name}_maskT", f"{name}_b2", f"{name}_psO{bk}"),
                    writes=(f"{name}_psO{bk}w",))
                tok_xo = f"{name}_xo{pbuf}_{bk}"
                pb.op("vector", lambda h, i_loc=i_loc, bk=bk, pbuf=pbuf: h.tensor_add(
                    out=xo[:, pbuf * D + bk * 512:pbuf * D + (bk + 1) * 512],
                    in0=psO[bk][:], in1=acc[:, i_loc * D + bk * 512:i_loc * D + (bk + 1) * 512]),
                    reads=(f"{name}_psO{bk}w", f"{name}_acc"),
                    writes=(f"{name}_psO{bk}", tok_xo + "t", tok_xo))
                pb.op("vector", lambda h, bk=bk, pbuf=pbuf: h.tensor_add(
                    out=xo[:, pbuf * D + bk * 512:pbuf * D + (bk + 1) * 512],
                    in0=xo[:, pbuf * D + bk * 512:pbuf * D + (bk + 1) * 512],
                    in1=xres[:, pbuf * D + bk * 512:pbuf * D + (bk + 1) * 512]),
                    reads=(tok_xo + "t", tok_xr), writes=(tok_xo,))
            pb.dma(xout_dram[i_glob * P:(i_glob + 1) * P, :], xo[:, pbuf * D:(pbuf + 1) * D],
                   reads=(f"{name}_xo{pbuf}_0", f"{name}_xo{pbuf}_1"),
                   writes=(f"{name}_stout{pbuf}",))
    pb.wait_all("sync", [f"{name}_stout0", f"{name}_stout1"])



def emit_softplus_evac(pb, ps, pstok, bias_ap, bias_tok, dst, dsttok, tmp, pre):
    n = dst.shape[-1]
    np_ = dst.shape[0]
    b1 = bias_ap if bias_ap is not None else 0.0
    breads = (bias_tok,) if bias_tok else ()
    pb.op("vector", lambda h: h.tensor_scalar(
        out=tmp[0:np_, 0:n], in0=ps, scalar1=b1, scalar2=None,
        op0=ALU.add),
        reads=(pstok,) + breads, writes=(pre + "s0",))
    onev, onetok = pb.constvec(1.0)
    pb.op("vector", lambda h: h.tensor_scalar(
        out=tmp[0:np_, n:2 * n], in0=tmp[0:np_, 0:n], scalar1=0.0, scalar2=None,
        op0=ALU.max),
        reads=(pre + "s0",), writes=(pre + "relu",))
    pb.op("scalar", lambda h: h.activation(out=tmp[0:np_, 0:n], in_=tmp[0:np_, 0:n],
                                           func=AF.Abs),
          reads=(pre + "s0", pre + "relu"), writes=(pre + "abs",))
    pb.op("scalar", lambda h: h.activation(out=tmp[0:np_, 0:n], in_=tmp[0:np_, 0:n],
                                           func=AF.Exp, scale=-1.0),
          reads=(pre + "abs",), writes=(pre + "exp",))
    pb.op("scalar", lambda h: h.activation(out=tmp[0:np_, 0:n], in_=tmp[0:np_, 0:n],
                                           func=AF.Ln, bias=onev[0:np_, :]),
          reads=(pre + "exp", onetok), writes=(pre + "ln",))
    pb.op("vector", lambda h: h.tensor_add(out=dst, in0=tmp[0:np_, 0:n],
                                           in1=tmp[0:np_, n:2 * n]),
          reads=(pre + "ln", pre + "relu"), writes=(dsttok, pstok + "free"))


def emit_qk_head(pb, lnTr, lnname, wsb, wname, bT, btok, hh, T, dst, dsttok, ps,
                 pstok, tmp, pre, softplus=True):
    wr = wsb.rearrange("p (c n) -> p c n", c=NCH)
    for tb in range(T // 512):
        for kc in range(NCH):
            pb.op("tensor", lambda h, kc=kc, tb=tb, hh=hh: h.matmul(
                out=ps[0:64, :], lhsT=wr[:, kc, hh * 64:(hh + 1) * 64],
                rhs=lnTr[:, kc, tb * 512:(tb + 1) * 512],
                start=(kc == 0), stop=(kc == NCH - 1)),
                reads=(f"{lnname}_lnT", wname, pstok + "free", pstok),
                writes=(pstok,) if kc == NCH - 1 else ())
        if softplus:
            emit_softplus_evac(pb, ps[0:64, :], pstok,
                               bT[0:64, hh:hh + 1] if bT is not None else None,
                               btok, dst[0:64, tb * 512:(tb + 1) * 512], dsttok,
                               tmp, pre)
        else:
            pb.op("vector", lambda h, tb=tb: h.tensor_copy(
                out=dst[0:64, tb * 512:(tb + 1) * 512], in_=ps[0:64, :]),
                reads=(pstok,), writes=(dsttok, pstok + "free"))


def emit_attn_b_scan(pb, x_dram, T, name, numden_dram, states_dram, ident):
    H = 16
    nck = T // P
    lnT, _ = emit_ln_lnT(pb, x_dram, T, "lnab", ident, want_T=True)
    lnTr = lnT.rearrange("p (c t) -> p c t", c=NCH)
    wq, _ = emit_load_wT(pb, f"{name}_wq", pb.dram("L0_attn_b_Wq_T", [D, D], F32,
                                                   kind="ExternalInput"), D, D)
    wk, _ = emit_load_wT(pb, f"{name}_wk", pb.dram("L0_attn_b_Wk_T", [D, D], F32,
                                                   kind="ExternalInput"), D, D)
    wv, _ = emit_load_wT(pb, f"{name}_wv", pb.dram("L0_attn_b_Wv_T", [D, D], F32,
                                                   kind="ExternalInput"), D, D)
    bqT = emit_bias_T(pb, f"{name}_bq", "L0_attn_b_bq", ident)
    bkT = emit_bias_T(pb, f"{name}_bk", "L0_attn_b_bk", ident)
    bvrep, _ = emit_bcast_row(pb, f"{name}_bvrep",
                              pb.dram("L0_attn_b_bv", [D], F32, kind="ExternalInput"),
                              D, pb._mem["ones1_sb"])
    trimask = pb.sbuf(f"{name}_trim", [P, P], BF16)
    pb.dma(trimask[:], pb.dram("c_triu", [P, P], BF16, kind="ExternalInput"),
           writes=(f"{name}_trim",))
    onecol_bf = pb.sbuf(f"{name}_onecol", [P, 1], BF16)
    pb.op("vector", lambda h: h.memset(onecol_bf[:], 1.0), writes=(f"{name}_onecolb",))

    qT = pb.sbuf(f"{name}_qT", [64, T], BF16)
    kT = pb.sbuf(f"{name}_kT", [64, T], BF16)
    vtok = pb.sbuf(f"{name}_vtok", [P, (T // P) * 64], BF16)
    stm = pb.sbuf(f"{name}_stm", [P, P], BF16)
    ktm = pb.sbuf(f"{name}_ktm", [P, 64], BF16)
    kvsb = pb.sbuf(f"{name}_kv", [64, 65], BF16)
    kvf = pb.sbuf(f"{name}_kvf", [64, 65], F32)
    ndsb = pb.sbuf(f"{name}_nd", [P, nck * 65], F32)
    stc = pb.sbuf(f"{name}_stc", [64, 65], F32)
    tmp = pb.sbuf(f"{name}_tmp", [P, 1024], F32)
    psP = pb.psum("ps_shared512", [P, 512], F32)
    psS = pb.psum(f"{name}_psS", [P, P], F32)
    psN = pb.psum(f"{name}_psN", [P, 65], F32)
    psKV = pb.psum(f"{name}_psKV", [64, 65], F32)
    psT = pb.psum(f"{name}_psT", [P, 64], BF16)

    for hh in range(H):
        emit_qk_head(pb, lnTr, "lnab", wq, f"{name}_wq", bqT, f"{name}_bq" + "T", hh, T,
                     qT, f"{name}_qT", psP, "ps_shared512", tmp, f"{name}_sq")
        emit_qk_head(pb, lnTr, "lnab", wk, f"{name}_wk", bkT, f"{name}_bk" + "T", hh, T,
                     kT, f"{name}_kT", psP, "ps_shared512", tmp, f"{name}_sk")
        wvr = wv.rearrange("p (c n) -> p c n", c=NCH)
        for i in range(nck):
            for kc in range(NCH):
                pb.op("tensor", lambda h, kc=kc, i=i, hh=hh: h.matmul(
                    out=psT.bitcast(F32)[:, 0:64] if False else psN[:, 0:64],
                    lhsT=lnTr[:, kc, i * P:(i + 1) * P],
                    rhs=wvr[:, kc, hh * 64:(hh + 1) * 64],
                    start=(kc == 0), stop=(kc == NCH - 1)),
                    reads=("lnab_lnT", f"{name}_wv", f"{name}_psN"),
                    writes=(f"{name}_psNw",) if kc == NCH - 1 else ())
            pb.op("vector", lambda h, i=i, hh=hh: h.tensor_add(
                out=vtok[:, i * 64:(i + 1) * 64], in0=psN[:, 0:64],
                in1=bvrep[:, hh * 64:(hh + 1) * 64]),
                reads=(f"{name}_psNw", f"{name}_bvrep"),
                writes=(f"{name}_psN", f"{name}_vtok",))
        pb.op("vector", lambda h: h.memset(kvsb[:], 0.0),
              reads=(), writes=(f"{name}_kvld",))
        pb.op("vector", lambda h: h.memset(kvf[:], 0.0),
              reads=(), writes=(f"{name}_kvf",))
        for c in range(nck):
            pb.op("tensor", lambda h, c=c: h.matmul(
                out=psS[:], lhsT=kT[:, c * P:(c + 1) * P], rhs=qT[:, c * P:(c + 1) * P],
                start=True, stop=True),
                reads=(f"{name}_qT", f"{name}_kT", f"{name}_psS"),
                writes=(f"{name}_psSw",))
            pb.op("vector", lambda h: h.tensor_mul(out=stm[:], in0=psS[:], in1=trimask[:]),
                  reads=(f"{name}_psSw", f"{name}_trim"), writes=(f"{name}_psS", f"{name}_stm"))
            pb.op("tensor", lambda h, c=c: h.matmul(
                out=psN[:], lhsT=qT[:, c * P:(c + 1) * P], rhs=kvsb[:],
                start=True, stop=False),
                reads=(f"{name}_qT", f"{name}_kvld", f"{name}_psN"), writes=())
            pb.op("tensor", lambda h, c=c: h.matmul(
                out=psN[:, 0:64], lhsT=stm[:], rhs=vtok[:, c * 64:(c + 1) * 64],
                start=False, stop=False),
                reads=(f"{name}_stm", f"{name}_vtok"), writes=())
            pb.op("tensor", lambda h, c=c: h.matmul(
                out=psN[:, 64:65], lhsT=stm[:], rhs=onecol_bf[:],
                start=False, stop=True),
                reads=(f"{name}_stm", f"{name}_onecolb"), writes=(f"{name}_psNw",))
            pb.op("vector", lambda h, c=c: h.tensor_copy(
                out=ndsb[:, c * 65:(c + 1) * 65], in_=psN[:]),
                reads=(f"{name}_psNw",), writes=(f"{name}_psN", f"{name}_nd"))
            pb.op("tensor", lambda h, c=c: h.transpose(
                out=psT[:, 0:64], in_=kT[:, c * P:(c + 1) * P], identity=ident[0:64, 0:64]),
                reads=(f"{name}_kT", "ident", f"{name}_psT"), writes=(f"{name}_psTw",))
            pb.op("scalar", lambda h: h.copy(out=ktm[:], in_=psT[:, 0:64]),
                  reads=(f"{name}_psTw",), writes=(f"{name}_psT", f"{name}_ktm"))
            pb.op("tensor", lambda h, c=c: h.matmul(
                out=psKV[:, 0:64], lhsT=ktm[:], rhs=vtok[:, c * 64:(c + 1) * 64],
                start=True, stop=False),
                reads=(f"{name}_ktm", f"{name}_vtok", f"{name}_psKV"), writes=())
            pb.op("tensor", lambda h, c=c: h.matmul(
                out=psKV[:, 64:65], lhsT=ktm[:], rhs=onecol_bf[:],
                start=False, stop=True),
                reads=(f"{name}_ktm", f"{name}_onecolb"), writes=(f"{name}_psKVw",))
            pb.op("vector", lambda h: h.tensor_add(out=kvf[:], in0=kvf[:], in1=psKV[:]),
                  reads=(f"{name}_psKVw", f"{name}_kvf"),
                  writes=(f"{name}_kvf", f"{name}_psKV"))
            pb.op("vector", lambda h: h.tensor_copy(out=kvsb[:], in_=kvf[:]),
                  reads=(f"{name}_kvf",), writes=(f"{name}_kvld",))
        pb.op("vector", lambda h: h.tensor_copy(out=stc[:], in_=kvf[:]),
              reads=(f"{name}_kvf",), writes=(f"{name}_stc",))
        pb.dma(numden_dram[hh].rearrange("(n p) e -> p n e", p=P), ndsb[:],
               reads=(f"{name}_nd",), writes=(f"{name}_ndst",), stream=f"st_{name}_nd")
        pb.dma(states_dram[hh], stc[:], reads=(f"{name}_stc",),
               writes=(f"{name}_stst",), stream=f"st_{name}_ss")
    pb.wait_all("sync", [f"{name}_ndst", f"{name}_stst"])


def emit_bias_T(pb, name, dram_name, ident):
    raw = pb.sbuf(name + "_raw", [1, D], F32)
    dr = pb.dram(dram_name, [D], F32, kind="ExternalInput")
    pb.dma(raw[:], dr.rearrange("(o n) -> o n", o=1), writes=(name + "_raw",))
    identf = pb._mem["identf_sb"]
    bT = pb.sbuf(name + "T", [64, 16], F32)
    ps = pb.psum("shared_bias_ps", [64, 16], F32)
    for hh in range(16):
        pb.op("tensor", lambda h, hh=hh: h.transpose(
            out=ps[:, hh:hh + 1], in_=raw[0:1, hh * 64:(hh + 1) * 64],
            identity=identf[0:1, 0:1]),
            reads=(name + "_raw", "identf", "sh_bias_ps"), writes=(f"{name}_psw{hh}",))
    pb.op("vector", lambda h: h.tensor_copy(out=bT[:], in_=ps[:]),
        reads=tuple(f"{name}_psw{hh}" for hh in range(16)),
        writes=("sh_bias_ps", name + "T"))
    return bT


def emit_attn_b_out(pb, x_dram, xout_dram, T, name, numden_dram, carry_dram, ident):
    H = 16
    nck = T // P
    lnT, _ = emit_ln_lnT(pb, x_dram, T, "lnab2", ident, want_T=True)
    lnTr = lnT.rearrange("p (c t) -> p c t", c=NCH)
    wq, _ = emit_load_wT(pb, f"{name}_wq", pb.dram("L0_attn_b_Wq_T", [D, D], F32,
                                                   kind="ExternalInput"), D, D)
    bqT = emit_bias_T(pb, f"{name}_bq", "L0_attn_b_bq", ident)
    borep, _ = emit_bcast_row(pb, f"{name}_borep",
                              pb.dram("L0_attn_b_bo", [D], F32, kind="ExternalInput"),
                              D, pb._mem["ones1_sb"])
    qT = pb.sbuf(f"{name}_qT", [64, T], BF16)
    carry = pb.sbuf(f"{name}_carry", [64, 65], BF16)
    ndsb = pb.sbuf(f"{name}_nd", [P, nck * 65], F32)
    numf = pb.sbuf(f"{name}_numf", [P, 65], F32)
    dinv = pb.sbuf(f"{name}_dinv", [P, 1], F32)
    otok = pb.sbuf(f"{name}_otok", [P, 64], BF16)
    outT = pb.sbuf(f"{name}_outT", [P, NCH * T], BF16)
    tmp = pb.sbuf(f"{name}_tmp", [P, 1024], F32)
    xres = pb.sbuf(f"{name}_xres", [P, 2 * D], F32)
    xo = pb.sbuf(f"{name}_xo", [P, 2 * D], F32)
    psP = pb.psum("ps_shared512", [P, 512], F32)
    psF = pb.psum(f"{name}_psF", [P, 65], F32)
    psT = pb.psum(f"{name}_psT", [P, P], BF16)
    psp = [pb.psum(f"{name}_psp{j}", [P, 512], F32) for j in range(2)]
    outTr = outT.rearrange("p (c t) -> p c t", c=NCH)

    for hh in range(H):
        emit_qk_head(pb, lnTr, "lnab2", wq, f"{name}_wq", bqT, f"{name}_bq" + "T", hh, T,
                     qT, f"{name}_qT", psP, "ps_shared512", tmp, f"{name}_sq")
        pb.dma(carry[:], carry_dram[hh], writes=(f"{name}_carry",), queue="gpsimd",
               stream=f"st_{name}_carry")
        pb.dma(ndsb[:], numden_dram[hh].rearrange("(n p) e -> p n e", p=P),
               reads=(f"{name}_ndr",), writes=(f"{name}_nd",), stream=f"st_{name}_nd")
        for c in range(nck):
            pb.op("tensor", lambda h, c=c: h.matmul(
                out=psF[:], lhsT=qT[:, c * P:(c + 1) * P], rhs=carry[:],
                start=True, stop=True),
                reads=(f"{name}_qT", f"{name}_carry", f"{name}_psF"),
                writes=(f"{name}_psFw",))
            pb.op("vector", lambda h, c=c: h.tensor_add(
                out=numf[:], in0=psF[:], in1=ndsb[:, c * 65:(c + 1) * 65]),
                reads=(f"{name}_psFw", f"{name}_nd"),
                writes=(f"{name}_psF", f"{name}_numf", f"{name}_ndr")),
            pb.op("vector", lambda h: h.reciprocal(out=dinv[:], in_=numf[:, 64:65]),
                  reads=(f"{name}_numf",), writes=(f"{name}_dinv",))
            pb.op("vector", lambda h: h.tensor_scalar_mul(
                out=otok[:], in0=numf[:, 0:64], scalar1=dinv[:]),
                reads=(f"{name}_numf", f"{name}_dinv"), writes=(f"{name}_otok",))
            r0 = (hh % 2) * 64
            pb.op("tensor", lambda h, c=c, r0=r0: h.transpose(
                out=psT[r0:r0 + 64, :], in_=otok[:], identity=ident[:]),
                reads=(f"{name}_otok", "ident", f"{name}_psT"), writes=(f"{name}_psTw",))
            pb.op("vector", lambda h, c=c, hh=hh, r0=r0: h.tensor_copy(
                out=outTr[r0:r0 + 64, hh // 2, c * P:(c + 1) * P],
                in_=psT[r0:r0 + 64, :]),
                reads=(f"{name}_psTw",), writes=(f"{name}_psT", f"{name}_outT"))
    wo, _ = emit_load_wT(pb, f"{name}_wq", pb.dram("L0_attn_b_Wo_T", [D, D], F32,
                                                   kind="ExternalInput"), D, D)
    wor = wo.rearrange("p (c n) -> p c n", c=NCH)
    for i in range(nck):
        pbuf = i % 2
        tok_xr = f"{name}_xr{pbuf}"
        pb.dma(xres[:, pbuf * D:(pbuf + 1) * D], x_dram[i * P:(i + 1) * P, :],
               reads=(tok_xr,), writes=(tok_xr,))
        for bk in range(2):
            for kc in range(NCH):
                pb.op("tensor", lambda h, kc=kc, bk=bk, i=i: h.matmul(
                    out=psp[bk][:], lhsT=outTr[:, kc, i * P:(i + 1) * P],
                    rhs=wor[:, kc, bk * 512:(bk + 1) * 512],
                    start=(kc == 0), stop=(kc == NCH - 1)),
                    reads=(f"{name}_outT", f"{name}_wq", f"{name}_psp{bk}"),
                    writes=(f"{name}_psp{bk}w",) if kc == NCH - 1 else ())
            tok_xo = f"{name}_xo{pbuf}_{bk}"
            pb.op("vector", lambda h, bk=bk, pbuf=pbuf: h.tensor_add(
                out=xo[:, pbuf * D + bk * 512:pbuf * D + (bk + 1) * 512],
                in0=psp[bk][:], in1=borep[:, bk * 512:(bk + 1) * 512]),
                reads=(f"{name}_psp{bk}w", f"{name}_borep"),
                writes=(f"{name}_psp{bk}", tok_xo + "t", tok_xo))
            pb.op("vector", lambda h, bk=bk, pbuf=pbuf: h.tensor_add(
                out=xo[:, pbuf * D + bk * 512:pbuf * D + (bk + 1) * 512],
                in0=xo[:, pbuf * D + bk * 512:pbuf * D + (bk + 1) * 512],
                in1=xres[:, pbuf * D + bk * 512:pbuf * D + (bk + 1) * 512]),
                reads=(tok_xo + "t", tok_xr), writes=(tok_xo,))
        pb.dma(xout_dram[i * P:(i + 1) * P, :], xo[:, pbuf * D:(pbuf + 1) * D],
               reads=(f"{name}_xo{pbuf}_0", f"{name}_xo{pbuf}_1"),
               writes=(f"{name}_stout{pbuf}",))
    pb.wait_all("sync", [f"{name}_stout0", f"{name}_stout1"])


def build_attn_b_scan_launch(T):
    nc = bass.Bass("TRN2", target_bir_lowering=False, debug=False)
    x = nc.dram_tensor("x", [T, D], F32, kind="ExternalInput").ap()
    nd = nc.dram_tensor("numden", [16, T, 65], F32, kind="ExternalOutput").ap()
    st = nc.dram_tensor("states", [16, 64, 65], F32, kind="ExternalOutput").ap()
    pb = PB(nc)
    ident = emit_const_ident(pb)
    ones1 = pb.sbuf("ones1_sb", [1, P], BF16)
    pb.dma(ones1[:], pb.dram("ones1", [1, P], BF16, kind="ExternalInput"),
           writes=("ones1",))
    emit_attn_b_scan(pb, x, T, "ab", nd, st, ident)
    pb.finalize()
    return nc


def build_attn_b_out_launch(T):
    nc = bass.Bass("TRN2", target_bir_lowering=False, debug=False)
    x = nc.dram_tensor("x", [T, D], F32, kind="ExternalInput").ap()
    nd = nc.dram_tensor("numden", [16, T, 65], F32, kind="ExternalInput").ap()
    carry = nc.dram_tensor("carry", [16, 64, 65], F32, kind="ExternalInput").ap()
    xout = nc.dram_tensor("xout", [T, D], F32, kind="ExternalOutput").ap()
    pb = PB(nc)
    ident = emit_const_ident(pb)
    ones1 = pb.sbuf("ones1_sb", [1, P], BF16)
    pb.dma(ones1[:], pb.dram("ones1", [1, P], BF16, kind="ExternalInput"),
           writes=("ones1",))
    emit_attn_b_out(pb, x, xout, T, "ao", nd, carry, ident)
    pb.finalize()
    return nc


def host_triu():
    m = np.triu(np.ones((P, P), np.float32)).astype(BF)
    return m



def emit_vec128T(pb, name, dram_name, ident, shape=None):
    raw = pb.sbuf(name + "_raw", [1, D], F32)
    dr = pb.dram(dram_name, shape if shape else [D], F32, kind="ExternalInput")
    rsrc = dr
    if shape and len(shape) == 3:
        rsrc = dr.rearrange("a b n -> a (b n)")[0]
    vsrc = rsrc.rearrange("(o n) -> o n", o=1) if len(rsrc.shape) == 1 else rsrc
    pb.dma(raw[:], vsrc, writes=(name + "_raw",))
    identf = pb._mem["identf_sb"]
    vT = pb.sbuf(name + "T", [P, NCH], F32)
    ps = pb.psum("shared_bias_ps", [64, 16], F32)
    for c in range(NCH):
        pb.op("tensor", lambda h, c=c: h.transpose(
            out=ps[0:64, c:c + 1] if False else ps[:, c:c + 1],
            in_=raw[0:1, c * P:(c + 1) * P], identity=identf[0:1, 0:1]),
            reads=(name + "_raw", "identf", "sh_bias_ps"), writes=(f"{name}_psw{c}",))
    return vT


def emit_vec128T2(pb, name, dram_name, ident, shape=None):
    raw = pb.sbuf(name + "_raw", [1, D], F32)
    dr = pb.dram(dram_name, shape if shape else [D], F32, kind="ExternalInput")
    rsrc = dr
    if shape is not None and len(shape) == 3:
        rsrc = dr.rearrange("a b n -> (a b n)")
    vsrc = rsrc.rearrange("(o n) -> o n", o=1) if len(rsrc.shape) == 1 else rsrc
    pb.dma(raw[:], vsrc, writes=(name + "_raw",))
    identf = pb._mem["identf_sb"]
    vT = pb.sbuf(name + "T", [P, NCH], F32)
    ps = pb.psum("sh_vec_ps", [P, NCH], F32)
    for c in range(NCH):
        pb.op("tensor", lambda h, c=c: h.transpose(
            out=ps[:, c:c + 1], in_=raw[0:1, c * P:(c + 1) * P],
            identity=identf[0:1, 0:1]),
            reads=(name + "_raw", "identf", "sh_vec_ps"), writes=(f"{name}_psw{c}",))
    pb.op("vector", lambda h: h.tensor_copy(out=vT[:], in_=ps[:]),
          reads=tuple(f"{name}_psw{c}" for c in range(NCH)),
          writes=("sh_vec_ps", name + "T"))
    return vT


def emit_rwkv(pb, x_dram, T, name, mode, xout_dram=None, states_dram=None,
              carry_dram=None, ident=None):
    ntile = T // P
    ntb = T // 512
    identf = pb._mem["identf_sb"]
    W1 = T + 1
    tdT = emit_vec128T2(pb, f"{name}_td", "L1_attn_b_time_decay", ident)
    tfT = emit_vec128T2(pb, f"{name}_tf", "L1_attn_b_time_first", ident)
    tmkT = emit_vec128T2(pb, f"{name}_tmk", "L1_attn_b_time_mix_key", ident, [1, 1, D])
    tmvT = emit_vec128T2(pb, f"{name}_tmv", "L1_attn_b_time_mix_value", ident, [1, 1, D])
    tmrT = emit_vec128T2(pb, f"{name}_tmr", "L1_attn_b_time_mix_receptance", ident, [1, 1, D])
    dT = pb.sbuf(f"{name}_dT", [P, NCH], F32)
    tfe = pb.sbuf(f"{name}_tfe", [P, NCH], F32)
    pb.op("scalar", lambda h: h.activation(out=dT[:], in_=tdT[:], func=AF.Exp),
          reads=(f"{name}_tdT",), writes=(f"{name}_dT0",))
    pb.op("scalar", lambda h: h.activation(out=dT[:], in_=dT[:], func=AF.Exp, scale=-1.0),
          reads=(f"{name}_dT0",), writes=(f"{name}_dT",))
    pb.op("scalar", lambda h: h.activation(out=tfe[:], in_=tfT[:], func=AF.Exp),
          reads=(f"{name}_tfT",), writes=(f"{name}_tfe",))
    lnX, _ = emit_ln_lnT(pb, x_dram, T, f"ln{name}", ident, want_T=True,
                         width=W1, colofs=1)
    lnXr = lnX.rearrange("p (c t) -> p c t", c=NCH)
    xprow = pb.sbuf(f"{name}_xprow", [1, D], F32)
    pb.dma(xprow[:], pb.dram("c_xprev", [1, D], F32, kind="ExternalInput"),
           writes=(f"{name}_xprow",))
    prst = pb.sbuf(f"{name}_prst", [1, 14], F32)
    pb.op("vector", lambda h: h.bn_stats(out=prst[0:1, 0:6], in_=xprow[0:1, 0:512]),
          reads=(f"{name}_xprow",), writes=(f"{name}_prsta",))
    pb.op("vector", lambda h: h.bn_stats(out=prst[0:1, 6:12], in_=xprow[0:1, 512:1024]),
          reads=(f"{name}_xprow",), writes=(f"{name}_prstb",))
    pb.op("vector", lambda h: h.bn_aggr(out=prst[0:1, 12:14], in_=prst[0:1, 0:12]),
          reads=(f"{name}_prsta", f"{name}_prstb"), writes=(f"{name}_prmv",))
    epsv, epstok = pb.constvec(EPS)
    pb.op("scalar", lambda h: h.activation(out=prst[0:1, 0:1], in_=prst[0:1, 13:14],
                                           func=AF.Sqrt, bias=epsv[0:1, :]),
          reads=(f"{name}_prmv", epstok), writes=(f"{name}_prsd",))
    pb.op("vector", lambda h: h.reciprocal(out=prst[0:1, 1:2], in_=prst[0:1, 0:1]),
          reads=(f"{name}_prsd",), writes=(f"{name}_pris",))
    pb.op("vector", lambda h: h.tensor_scalar(
        out=xprow[:], in0=xprow[:], scalar1=prst[0:1, 12:13], scalar2=prst[0:1, 1:2],
        op0=ALU.subtract, op1=ALU.mult),
        reads=(f"{name}_xprow", f"{name}_prmv", f"{name}_pris"),
        writes=(f"{name}_xprow",))
    psv = pb.psum("sh_vec_ps", [P, NCH], F32)
    for c in range(NCH):
        pb.op("tensor", lambda h, c=c: h.transpose(
            out=psv[:, c:c + 1], in_=xprow[0:1, c * P:(c + 1) * P],
            identity=identf[0:1, 0:1]),
            reads=(f"{name}_xprow", "identf", "sh_vec_ps"), writes=(f"{name}_pvw{c}",))
    pb.op("vector", lambda h: h.tensor_copy(
        out=lnXr[:, :, 0:1], in_=psv.rearrange("p (c o) -> p c o", o=1)),
        reads=tuple(f"{name}_pvw{c}" for c in range(NCH)),
        writes=("sh_vec_ps", f"{name}_bnd"))
    kvr_d = pb.dram(f"{name}_kvr", [3, D, T], BF16, kind="Internal")
    wbuf = pb.sbuf(f"{name}_w", [P, NCH * D], BF16)
    xm = pb.sbuf(f"{name}_xm", [P, NCH * 512], BF16)
    dtm = pb.sbuf(f"{name}_dtm", [P, 512], F32)
    pst = [pb.psum(f"{name}_pst{j}", [P, 512], F32) for j in range(2)]
    evb = pb.sbuf(f"{name}_evb", [P, 2 * 512], BF16)
    projs = [("L1_attn_b_w_k_T", tmkT, f"{name}_tmkT", 0),
             ("L1_attn_b_w_v_T", tmvT, f"{name}_tmvT", 1)]
    if mode == "out":
        projs.append(("L1_attn_b_w_r_T", tmrT, f"{name}_tmrT", 2))
    for wnm, tmT, tmtok, slot in projs:
        emit_load_wT(pb, f"{name}_w", pb.dram(wnm, [D, D], F32, kind="ExternalInput"),
                     D, D)
        wr = wbuf.rearrange("p (c n) -> p c n", c=NCH)
        for tb in range(ntb):
            t0 = tb * 512
            for c in range(NCH):
                pb.op("vector", lambda h, c=c, t0=t0: h.tensor_sub(
                    out=dtm[:], in0=lnXr[:, c, 1 + t0:1 + t0 + 512],
                    in1=lnXr[:, c, t0:t0 + 512]),
                    reads=(f"ln{name}_lnT", f"{name}_bnd"), writes=(f"{name}_dtm",))
                pb.op("vector", lambda h, c=c, t0=t0, tmT=tmT: h.scalar_tensor_tensor(
                    out=xm[:, c * 512:(c + 1) * 512], in0=dtm[:],
                    scalar=tmT[:, c:c + 1], in1=lnXr[:, c, t0:t0 + 512],
                    op0=ALU.mult, op1=ALU.add),
                    reads=(f"{name}_dtm", tmtok, f"ln{name}_lnT", f"{name}_bnd"),
                    writes=(f"{name}_xm{c}",))
            xmr = xm.rearrange("p (c n) -> p c n", c=NCH)
            for oc in range(NCH):
                jf = oc % 2
                for kc in range(NCH):
                    pb.op("tensor", lambda h, oc=oc, kc=kc, jf=jf, wr=wr: h.matmul(
                        out=pst[jf][:], lhsT=wr[:, kc, oc * P:(oc + 1) * P],
                        rhs=xmr[:, kc, :], start=(kc == 0), stop=(kc == NCH - 1)),
                        reads=(f"{name}_w",) + tuple(f"{name}_xm{c2}" for c2 in range(NCH)) + (f"{name}_pst{jf}",),
                        writes=(f"{name}_pst{jf}w",) if kc == NCH - 1 else ())
                pb.op("vector", lambda h, jf=jf: h.tensor_copy(
                    out=evb[:, jf * 512:(jf + 1) * 512], in_=pst[jf][:]),
                    reads=(f"{name}_pst{jf}w", f"{name}_evb{jf}"),
                    writes=(f"{name}_pst{jf}", f"{name}_evb{jf}"))
                pb.dma(kvr_d[slot, oc * P:(oc + 1) * P, t0:t0 + 512],
                       evb[:, jf * 512:(jf + 1) * 512],
                       reads=(f"{name}_evb{jf}",),
                       writes=(f"{name}_evb{jf}",),
                       writes_final=(f"{name}_kvrd{jf}",),
                       stream=f"st_{name}_kvr{jf}")
    ones512 = pb.sbuf(f"{name}_ones512", [P, 512], F32)
    zeros512 = pb.sbuf(f"{name}_zeros512", [P, 512], F32)
    pb.op("vector", lambda h: h.memset(ones512[:], 1.0), writes=(f"{name}_ones512",))
    pb.op("vector", lambda h: h.memset(zeros512[:], 0.0), writes=(f"{name}_zeros512",))
    kts = pb.sbuf(f"{name}_kts", [P, T], BF16)
    vts = pb.sbuf(f"{name}_vts", [P, T], BF16)
    rts = pb.sbuf(f"{name}_rts", [P, T], BF16) if mode == "out" else None
    drep = pb.sbuf(f"{name}_drep", [P, 512], F32)
    wexp = pb.sbuf(f"{name}_wexp", [P, 512], F32)
    ubuf = pb.sbuf(f"{name}_ubuf", [P, 512], F32)
    SA = pb.sbuf(f"{name}_SA", [P, 513], F32)
    SB = pb.sbuf(f"{name}_SB", [P, 513], F32)
    stq = pb.sbuf(f"{name}_stq", [P, 4], F32)
    states = pb.sbuf(f"{name}_states", [P, NCH * 2], F32)
    if mode == "out":
        dpow = pb.sbuf(f"{name}_dpow", [P, 513], F32)
        carr = pb.sbuf(f"{name}_carr", [P, NCH * 2], F32)
        pb.dma(carr[:], carry_dram.rearrange("(c p) s -> p c s", p=P),
               writes=(f"{name}_carr",))
        numb = pb.sbuf(f"{name}_numb", [P, 512], F32)
        denb = pb.sbuf(f"{name}_denb", [P, 512], F32)
        outtile = pb.sbuf(f"{name}_outtile", [P, 512], BF16)
        outd = pb.dram(f"{name}_outd", [D, T], BF16, kind="Internal")
    for c in range(NCH):
        pb.dma(kts[:], kvr_d[0, c * P:(c + 1) * P, :],
               reads=(f"{name}_kts", f"{name}_kvrd0", f"{name}_kvrd1"),
               writes=(f"{name}_kts",), stream=f"st_{name}_kts")
        pb.dma(vts[:], kvr_d[1, c * P:(c + 1) * P, :],
               reads=(f"{name}_vts", f"{name}_kvrd0", f"{name}_kvrd1"),
               writes=(f"{name}_vts",), stream=f"st_{name}_vts")
        if mode == "out":
            pb.dma(rts[:], kvr_d[2, c * P:(c + 1) * P, :],
                   reads=(f"{name}_rts", f"{name}_kvrd0", f"{name}_kvrd1"),
                   writes=(f"{name}_rts",), stream=f"st_{name}_rts")
        pb.op("vector", lambda h, c=c: h.tensor_scalar_mul(
            out=drep[:], in0=ones512[:], scalar1=dT[:, c:c + 1]),
            reads=(f"{name}_ones512", f"{name}_dT", f"{name}_drep"),
            writes=(f"{name}_drep",))
        pb.op("vector", lambda h: h.memset(stq[:], 0.0), reads=(f"{name}_stq",),
              writes=(f"{name}_stq",))
        if mode == "out":
            pb.op("vector", lambda h: h.memset(dpow[:, 0:1], 1.0),
                  reads=(f"{name}_dpow",), writes=(f"{name}_dpow0",))
            pb.op("vector", lambda h: h.tensor_tensor_scan(
                out=dpow[:, 1:513], data0=drep[:], data1=zeros512[:],
                initial=dpow[:, 0:1], op0=ALU.mult, op1=ALU.add),
                reads=(f"{name}_drep", f"{name}_zeros512", f"{name}_dpow0"),
                writes=(f"{name}_dpow",))
            pb.op("vector", lambda h, c=c: h.tensor_copy(
                out=stq[:, 2:4], in_=carr[:, c * 2:(c + 1) * 2]),
                reads=(f"{name}_carr", f"{name}_stq"), writes=(f"{name}_stq",))
        for tb in range(ntb):
            t0 = tb * 512
            pb.op("scalar", lambda h, t0=t0: h.activation(
                out=wexp[:], in_=kts[:, t0:t0 + 512], func=AF.Exp),
                reads=(f"{name}_kts", f"{name}_wexp"), writes=(f"{name}_wexp",))
            pb.op("vector", lambda h, t0=t0: h.tensor_mul(
                out=ubuf[:], in0=wexp[:], in1=vts[:, t0:t0 + 512]),
                reads=(f"{name}_wexp", f"{name}_vts", f"{name}_ubuf"),
                writes=(f"{name}_ubuf",))
            pb.op("vector", lambda h: h.tensor_copy(out=SA[:, 0:1], in_=stq[:, 0:1]),
                  reads=(f"{name}_stq", f"{name}_SA"), writes=(f"{name}_SA",))
            pb.op("vector", lambda h: h.tensor_copy(out=SB[:, 0:1], in_=stq[:, 1:2]),
                  reads=(f"{name}_stq", f"{name}_SB"), writes=(f"{name}_SB",))
            pb.op("vector", lambda h: h.tensor_tensor_scan(
                out=SA[:, 1:513], data0=drep[:], data1=ubuf[:],
                initial=SA[:, 0:1], op0=ALU.mult, op1=ALU.add),
                reads=(f"{name}_drep", f"{name}_ubuf", f"{name}_SA"),
                writes=(f"{name}_SA",))
            pb.op("vector", lambda h: h.tensor_tensor_scan(
                out=SB[:, 1:513], data0=drep[:], data1=wexp[:],
                initial=SB[:, 0:1], op0=ALU.mult, op1=ALU.add),
                reads=(f"{name}_drep", f"{name}_wexp", f"{name}_SB"),
                writes=(f"{name}_SB",))
            pb.op("vector", lambda h: h.tensor_copy(out=stq[:, 0:1], in_=SA[:, 512:513]),
                  reads=(f"{name}_SA", f"{name}_stq"), writes=(f"{name}_stq",))
            pb.op("vector", lambda h: h.tensor_copy(out=stq[:, 1:2], in_=SB[:, 512:513]),
                  reads=(f"{name}_SB", f"{name}_stq"), writes=(f"{name}_stq",))
            if mode == "out":
                pb.op("vector", lambda h: h.scalar_tensor_tensor(
                    out=numb[:], in0=dpow[:, 0:512], scalar=stq[:, 2:3],
                    in1=SA[:, 0:512], op0=ALU.mult, op1=ALU.add),
                    reads=(f"{name}_dpow", f"{name}_stq", f"{name}_SA", f"{name}_numb"),
                    writes=(f"{name}_numb",))
                pb.op("vector", lambda h, c=c: h.scalar_tensor_tensor(
                    out=numb[:], in0=ubuf[:], scalar=tfe[:, c:c + 1],
                    in1=numb[:], op0=ALU.mult, op1=ALU.add),
                    reads=(f"{name}_ubuf", f"{name}_tfe", f"{name}_numb"),
                    writes=(f"{name}_numb",))
                pb.op("vector", lambda h: h.scalar_tensor_tensor(
                    out=denb[:], in0=dpow[:, 0:512], scalar=stq[:, 3:4],
                    in1=SB[:, 0:512], op0=ALU.mult, op1=ALU.add),
                    reads=(f"{name}_dpow", f"{name}_stq", f"{name}_SB", f"{name}_denb"),
                    writes=(f"{name}_denb",))
                pb.op("vector", lambda h, c=c: h.scalar_tensor_tensor(
                    out=denb[:], in0=wexp[:], scalar=tfe[:, c:c + 1],
                    in1=denb[:], op0=ALU.mult, op1=ALU.add),
                    reads=(f"{name}_wexp", f"{name}_tfe", f"{name}_denb"),
                    writes=(f"{name}_denb",))
                pb.op("vector", lambda h: h.reciprocal(out=denb[:], in_=denb[:]),
                      reads=(f"{name}_denb",), writes=(f"{name}_denb",))
                pb.op("vector", lambda h: h.tensor_mul(out=numb[:], in0=numb[:], in1=denb[:]),
                      reads=(f"{name}_numb", f"{name}_denb"), writes=(f"{name}_numb",))
                pb.op("scalar", lambda h, t0=t0: h.activation(
                    out=denb[:], in_=rts[:, t0:t0 + 512], func=AF.Sigmoid),
                    reads=(f"{name}_rts", f"{name}_denb"), writes=(f"{name}_denb",))
                pb.op("vector", lambda h: h.tensor_mul(out=outtile[:], in0=numb[:], in1=denb[:]),
                      reads=(f"{name}_numb", f"{name}_denb", f"{name}_outtile"),
                      writes=(f"{name}_outtile",))
                pb.dma(outd[c * P:(c + 1) * P, t0:t0 + 512], outtile[:],
                       reads=(f"{name}_outtile",),
                       writes=(f"{name}_outtile",),
                       writes_final=(f"{name}_outdd",),
                       stream=f"st_{name}_outd")
                pb.op("vector", lambda h: h.tensor_mul(
                    out=stq[:, 2:4], in0=stq[:, 2:4],
                    in1=dpow[:, 512:513].broadcast_to([P, 2])),
                    reads=(f"{name}_dpow", f"{name}_stq"), writes=(f"{name}_stq",))
        if mode == "scan":
            pb.op("vector", lambda h, c=c: h.tensor_copy(
                out=states[:, c * 2:(c + 1) * 2], in_=stq[:, 0:2]),
                reads=(f"{name}_stq",), writes=(f"{name}_states",))
    if mode == "scan":
        pb.dma(states_dram.rearrange("(c p) s -> p c s", p=P), states[:],
               reads=(f"{name}_states",), writes=(f"{name}_stst",))
        pb.wait_all("sync", [f"{name}_stst"])
        return
    emit_load_wT(pb, f"{name}_w", pb.dram("L1_attn_b_w_o_T", [D, D], F32,
                                          kind="ExternalInput"), D, D)
    wr = wbuf.rearrange("p (c n) -> p c n", c=NCH)
    ott = pb.sbuf(f"{name}_ott", [P, 2 * D], BF16)
    xres = pb.sbuf(f"{name}_xres", [P, 2 * D], F32)
    xo = pb.sbuf(f"{name}_xo", [P, 2 * D], F32)
    psp = [pb.psum(f"{name}_psp{j}", [P, 512], F32) for j in range(2)]
    for i in range(ntile):
        pbuf = i % 2
        tok_ot = f"{name}_ott{pbuf}"
        pb.dma(ott[:, pbuf * D:(pbuf + 1) * D],
               outd[:, i * P:(i + 1) * P].rearrange("(c p) t -> p c t", p=P),
               reads=(tok_ot, f"{name}_outdd"), writes=(tok_ot,),
               stream=f"st_{name}_ott{pbuf}")
        tok_xr = f"{name}_xr{pbuf}"
        pb.dma(xres[:, pbuf * D:(pbuf + 1) * D], x_dram[i * P:(i + 1) * P, :],
               reads=(tok_xr,), writes=(tok_xr,))
        otr = ott.rearrange("p (b c t) -> p b c t", b=2, c=NCH)
        for bk in range(2):
            for kc in range(NCH):
                pb.op("tensor", lambda h, kc=kc, bk=bk, pbuf=pbuf, otr=otr: h.matmul(
                    out=psp[bk][:], lhsT=otr[:, pbuf, kc, :],
                    rhs=wr[:, kc, bk * 512:(bk + 1) * 512],
                    start=(kc == 0), stop=(kc == NCH - 1)),
                    reads=(tok_ot, f"{name}_w", f"{name}_psp{bk}"),
                    writes=(f"{name}_psp{bk}w",) if kc == NCH - 1 else ())
            tok_xo = f"{name}_xo{pbuf}_{bk}"
            pb.op("vector", lambda h, bk=bk, pbuf=pbuf: h.tensor_add(
                out=xo[:, pbuf * D + bk * 512:pbuf * D + (bk + 1) * 512],
                in0=psp[bk][:], in1=xres[:, pbuf * D + bk * 512:pbuf * D + (bk + 1) * 512]),
                reads=(f"{name}_psp{bk}w", tok_xr),
                writes=(f"{name}_psp{bk}", tok_xo, f"{name}_xo{pbuf}"))
        pb.dma(xout_dram[i * P:(i + 1) * P, :], xo[:, pbuf * D:(pbuf + 1) * D],
               reads=(f"{name}_xo{pbuf}_0", f"{name}_xo{pbuf}_1"),
               writes=(f"{name}_stout{pbuf}",))
    pb.wait_all("sync", [f"{name}_stout0", f"{name}_stout1"])


def build_rwkv_launch(T, mode):
    nc = bass.Bass("TRN2", target_bir_lowering=False, debug=False)
    x = nc.dram_tensor("x", [T, D], F32, kind="ExternalInput").ap()
    pb = PB(nc)
    ident = emit_const_ident(pb)
    if mode == "scan":
        st = nc.dram_tensor("states", [D, 2], F32, kind="ExternalOutput").ap()
        emit_rwkv(pb, x, T, "rw", "scan", states_dram=st, ident=ident)
    else:
        carry = nc.dram_tensor("carry", [D, 2], F32, kind="ExternalInput").ap()
        xout = nc.dram_tensor("xout", [T, D], F32, kind="ExternalOutput").ap()
        emit_rwkv(pb, x, T, "rw", "out", xout_dram=xout, carry_dram=carry, ident=ident)
    pb.finalize()
    return nc



LAM = 0.1 * 2.0 / 4096.0


def emit_ttt_proj(pb, x_dram, T, name, qkv_dram, gate_dram, ident):
    lnT, _ = emit_ln_lnT(pb, x_dram, T, f"ln{name}", ident, want_T=True)
    lnTr = lnT.rearrange("p (c t) -> p c t", c=NCH)
    ntb = T // 512
    wbuf = pb.sbuf(f"{name}_w", [P, NCH * D], BF16)
    evb = pb.sbuf(f"{name}_evb", [P, 2 * 512], BF16)
    pst = [pb.psum(f"{name}_pst{j}", [P, 512], F32) for j in range(2)]
    for wnm, slot in (("L1_attn_a_Wq_T", 0), ("L1_attn_a_Wk_T", 1), ("L1_attn_a_Wv_T", 2)):
        emit_load_wT(pb, f"{name}_w", pb.dram(wnm, [D, D], F32, kind="ExternalInput"),
                     D, D)
        wr = wbuf.rearrange("p (c n) -> p c n", c=NCH)
        for tb in range(ntb):
            t0 = tb * 512
            for oc in range(NCH):
                jf = oc % 2
                for kc in range(NCH):
                    pb.op("tensor", lambda h, oc=oc, kc=kc, jf=jf, wr=wr, t0=t0: h.matmul(
                        out=pst[jf][:], lhsT=wr[:, kc, oc * P:(oc + 1) * P],
                        rhs=lnTr[:, kc, t0:t0 + 512], start=(kc == 0), stop=(kc == NCH - 1)),
                        reads=(f"{name}_w", f"ln{name}_lnT", f"{name}_pst{jf}"),
                        writes=(f"{name}_pst{jf}w",) if kc == NCH - 1 else ())
                pb.op("vector", lambda h, jf=jf: h.tensor_copy(
                    out=evb[:, jf * 512:(jf + 1) * 512], in_=pst[jf][:]),
                    reads=(f"{name}_pst{jf}w", f"{name}_evb{jf}"),
                    writes=(f"{name}_pst{jf}", f"{name}_evb{jf}"))
                pb.dma(qkv_dram[slot, oc * P:(oc + 1) * P, t0:t0 + 512],
                       evb[:, jf * 512:(jf + 1) * 512],
                       reads=(f"{name}_evb{jf}",),
                       writes=(f"{name}_evb{jf}",),
                       writes_final=(f"{name}_qkvd{jf}",),
                       stream=f"st_{name}_qkv{jf}")
    emit_load_wT(pb, f"{name}_w", pb.dram("L1_attn_a_Wg_T", [D, D], F32,
                                          kind="ExternalInput"), D, D)
    wr = wbuf.rearrange("p (c n) -> p c n", c=NCH)
    xg = pb.sbuf(f"{name}_xg", [P, 512], F32)
    t2b = pb.sbuf(f"{name}_t2b", [P, 512], F32)
    gt = pb.sbuf(f"{name}_gt", [P, 512], BF16)
    SQ2PI = 0.7978845608028654
    for i in range(T // P):
        for bk in range(2):
            jf = bk
            for kc in range(NCH):
                pb.op("tensor", lambda h, kc=kc, bk=bk, i=i, jf=jf, wr=wr: h.matmul(
                    out=pst[jf][:], lhsT=lnTr[:, kc, i * P:(i + 1) * P],
                    rhs=wr[:, kc, bk * 512:(bk + 1) * 512],
                    start=(kc == 0), stop=(kc == NCH - 1)),
                    reads=(f"{name}_w", f"ln{name}_lnT", f"{name}_pst{jf}"),
                    writes=(f"{name}_pst{jf}w",) if kc == NCH - 1 else ())
            pb.op("vector", lambda h, jf=jf: h.tensor_copy(out=xg[:], in_=pst[jf][:]),
                  reads=(f"{name}_pst{jf}w", f"{name}_xg"),
                  writes=(f"{name}_pst{jf}", f"{name}_xg"))
            pb.op("vector", lambda h: h.tensor_mul(out=t2b[:], in0=xg[:], in1=xg[:]),
                  reads=(f"{name}_xg",), writes=(f"{name}_t2b",))
            pb.op("vector", lambda h: h.tensor_scalar(
                out=t2b[:], in0=t2b[:], scalar1=0.044715, scalar2=1.0,
                op0=ALU.mult, op1=ALU.add),
                reads=(f"{name}_t2b",), writes=(f"{name}_t2b",))
            pb.op("vector", lambda h: h.tensor_mul(out=t2b[:], in0=t2b[:], in1=xg[:]),
                  reads=(f"{name}_t2b", f"{name}_xg"), writes=(f"{name}_t2b",))
            pb.op("scalar", lambda h: h.activation(out=t2b[:], in_=t2b[:],
                                                   func=AF.Tanh, scale=SQ2PI),
                  reads=(f"{name}_t2b",), writes=(f"{name}_t2b",))
            pb.op("vector", lambda h: h.tensor_scalar(
                out=t2b[:], in0=t2b[:], scalar1=1.0, scalar2=0.5,
                op0=ALU.add, op1=ALU.mult),
                reads=(f"{name}_t2b",), writes=(f"{name}_t2b",))
            pb.op("vector", lambda h: h.tensor_mul(out=gt[:], in0=t2b[:], in1=xg[:]),
                  reads=(f"{name}_t2b", f"{name}_xg", f"{name}_gt"), writes=(f"{name}_gt",))
            pb.dma(gate_dram[i * P:(i + 1) * P, bk * 512:(bk + 1) * 512], gt[:],
                   reads=(f"{name}_gt",), writes=(f"{name}_gt",),
                   writes_final=(f"{name}_gted",), stream=f"st_{name}_gt")
    pb.wait_all("sync", [f"{name}_qkvd0", f"{name}_qkvd1", f"{name}_gted"])


def emit_ttt_scan(pb, name, nsteps, qs_dram, ks_dram, vs_dram, out_dram, chunk=1024):
    W0 = pb.sbuf(f"{name}_W0", [P, 64], F32)
    p30 = pb.sbuf(f"{name}_p30", [P, 3], F32)
    W2 = pb.sbuf(f"{name}_W2", [P, P], F32)
    p3 = pb.sbuf(f"{name}_p3", [P, 3], F32)
    gb = pb.sbuf(f"{name}_gb", [P, 2], F32)
    pavg = pb.sbuf(f"{name}_pavg", [P, P], F32)
    pb.dma(W0[:], pb.dram("c_ttt_W", [P, 64], F32, kind="ExternalInput"),
           writes=(f"{name}_W0",))
    pb.dma(p3[:], pb.dram("c_ttt_p3", [P, 3], F32, kind="ExternalInput"),
           writes=(f"{name}_p3",))
    pb.dma(gb[:], pb.dram("c_ttt_gb", [P, 2], F32, kind="ExternalInput"),
           writes=(f"{name}_gb",))
    pb.dma(pavg[:], pb.dram("c_pavg", [P, P], F32, kind="ExternalInput"),
           writes=(f"{name}_pavg",))
    pb.op("vector", lambda h: h.memset(W2[:], 0.0), writes=(f"{name}_W2",))
    pb.op("vector", lambda h: h.tensor_copy(out=W2[0:64, 0:64], in_=W0[0:64, :]),
          reads=(f"{name}_W0", f"{name}_W2"), writes=(f"{name}_W2",))
    pb.op("vector", lambda h: h.tensor_copy(out=W2[64:128, 64:128], in_=W0[64:128, :]),
          reads=(f"{name}_W0", f"{name}_W2"), writes=(f"{name}_W2",))
    epsv, epstok = pb.constvec(EPS)

    kch = pb.sbuf(f"{name}_kch", [P, 2 * 4 * chunk], BF16)
    vch = pb.sbuf(f"{name}_vch", [P, 2 * 4 * chunk], BF16)
    qch = pb.sbuf(f"{name}_qch", [P, 2 * 4 * chunk], BF16)
    och = pb.sbuf(f"{name}_och", [P, 2 * 4 * chunk], BF16)
    nm = lambda s: f"{name}_{s}"
    wt = {s: pb.sbuf(nm(s), [P, 4], F32) for s in
          ("kTf", "vmk", "h1b", "cen", "sq", "std", "istd", "xhat", "blv",
           "t1", "t2", "u", "w1", "w2b", "xh2", "cen2", "std2", "istd2", "zb")}
    scr = pb.sbuf(nm("scr"), [P, 12], F32)
    red3 = pb.sbuf(nm("red3"), [P, 3], F32)
    ktm = pb.sbuf(nm("ktm"), [4, P], F32)
    dtm = pb.sbuf(nm("dtm"), [4, P], F32)
    psA = pb.psum(nm("psA"), [P, 16], F32)
    psC = pb.psum(nm("psC"), [P, 16], F32)
    psD = pb.psum(nm("psD"), [P, P], F32)
    psE = pb.psum(nm("psE"), [4, 256], F32)
    identf = pb._mem["identf_sb"]

    nch_ = nsteps // chunk
    CW = 4 * chunk

    def ln_stats(src_tok, srcap, pre, ostd, oistd, mu_col):
        pb.op("tensor", lambda h: h.matmul(
            out=psC[:, mu_col:mu_col + 4], lhsT=pavg[:], rhs=srcap,
            start=True, stop=True),
            reads=(src_tok, nm("pavg"), nm("psC")), writes=(nm("psCmu"),))
        pb.op("vector", lambda h: h.tensor_sub(out=wt["cen"][:], in0=srcap,
                                               in1=psC[:, mu_col:mu_col + 4]),
              reads=(nm("psCmu"), src_tok), writes=(nm("cen"), nm("psC")))
        pb.op("scalar", lambda h: h.activation(out=wt["sq"][:], in_=wt["cen"][:],
                                               func=AF.Square),
              reads=(nm("cen"),), writes=(nm("sq"),))
        pb.op("tensor", lambda h: h.matmul(
            out=psC[:, mu_col + 4:mu_col + 8], lhsT=pavg[:], rhs=wt["sq"][:],
            start=True, stop=True),
            reads=(nm("sq"), nm("pavg"), nm("psC")), writes=(nm("psCvar"),))
        pb.op("scalar", lambda h: h.activation(out=wt[ostd][:],
                                               in_=psC[:, mu_col + 4:mu_col + 8],
                                               func=AF.Sqrt, bias=epsv[:]),
              reads=(nm("psCvar"), epstok), writes=(nm(ostd), nm("psC")))
        pb.op("vector", lambda h: h.reciprocal(out=wt[oistd][:], in_=wt[ostd][:]),
              reads=(nm(ostd),), writes=(nm(oistd),))

    for ci in range(nch_):
        cb = ci % 2
        ctok = f"{name}_ch{cb}"
        pb.dma(kch[:, cb * CW:(cb + 1) * CW], ks_dram[:, ci * CW:(ci + 1) * CW],
               reads=(ctok + "k",), writes=(ctok + "k",), stream=f"st_{name}_k{cb}")
        pb.dma(vch[:, cb * CW:(cb + 1) * CW], vs_dram[:, ci * CW:(ci + 1) * CW],
               reads=(ctok + "v",), writes=(ctok + "v",), stream=f"st_{name}_v{cb}")
        pb.dma(qch[:, cb * CW:(cb + 1) * CW], qs_dram[:, ci * CW:(ci + 1) * CW],
               reads=(ctok + "q",), writes=(ctok + "q",), stream=f"st_{name}_q{cb}")
        for tt in range(chunk):
            col = cb * CW + tt * 4
            kT = kch[:, col:col + 4]
            vT = vch[:, col:col + 4]
            qT = qch[:, col:col + 4]
            pb.op("vector", lambda h, kT=kT: h.tensor_copy(out=wt["kTf"][:], in_=kT),
                  reads=(ctok + "k",), writes=(nm("kTf"),))
            pb.op("vector", lambda h, kT=kT, vT=vT: h.tensor_sub(
                out=wt["vmk"][:], in0=vT, in1=kT),
                reads=(ctok + "k", ctok + "v"), writes=(nm("vmk"),))
            pb.op("tensor", lambda h: h.transpose(
                out=psE[:, 0:128], in_=wt["kTf"][:], identity=identf[:]),
                reads=(nm("kTf"), "identf", nm("psE")), writes=(nm("psEk"),))
            pb.op("scalar", lambda h: h.copy(out=ktm[:], in_=psE[:, 0:128]),
                  reads=(nm("psEk"),), writes=(nm("ktm"), nm("psE")))
            for it in range(3):
                pb.op("tensor", lambda h: h.matmul(
                    out=psA[:, 0:4], lhsT=W2[:], rhs=wt["kTf"][:],
                    start=True, stop=True),
                    reads=(nm("W2"), nm("kTf"), nm("psA")), writes=(nm("psAw"),))
                pb.op("scalar", lambda h: h.activation(
                    out=wt["h1b"][:], in_=psA[:, 0:4], func=AF.Identity,
                    bias=p3[:, 0:1]),
                    reads=(nm("psAw"), nm("p3")), writes=(nm("h1b"), nm("psA")))
                ln_stats(nm("h1b"), wt["h1b"][:], "l1", "std", "istd", 0)
                pb.op("vector", lambda h: h.tensor_mul(
                    out=wt["xhat"][:], in0=wt["cen"][:], in1=wt["istd"][:]),
                    reads=(nm("cen"), nm("istd")), writes=(nm("xhat"),))
                if it == 2:
                    pb.op("vector", lambda h: h.tensor_scalar(
                        out=wt["zb"][:], in0=wt["xhat"][:], scalar1=p3[:, 1:2],
                        scalar2=None, op0=ALU.mult),
                        reads=(nm("xhat"), nm("p3")), writes=(nm("zb"),))
                    pb.op("vector", lambda h: h.tensor_scalar(
                        out=wt["zb"][:], in0=wt["zb"][:], scalar1=p3[:, 2:3],
                        scalar2=None, op0=ALU.add),
                        reads=(nm("zb"), nm("p3")), writes=(nm("zb"),))
                    pb.op("tensor", lambda h: h.matmul(
                        out=psC[:, 8:12], lhsT=pavg[:], rhs=wt["zb"][:],
                        start=True, stop=True),
                        reads=(nm("zb"), nm("pavg"), nm("psC")), writes=(nm("psCm2u"),))
                    pb.op("vector", lambda h: h.tensor_sub(
                        out=wt["cen2"][:], in0=wt["zb"][:], in1=psC[:, 8:12]),
                        reads=(nm("psCm2u"), nm("zb")), writes=(nm("cen2"), nm("psC")))
                    pb.op("scalar", lambda h: h.activation(out=wt["sq"][:],
                                                           in_=wt["cen2"][:],
                                                           func=AF.Square),
                          reads=(nm("cen2"),), writes=(nm("sq"),))
                    pb.op("tensor", lambda h: h.matmul(
                        out=psC[:, 12:16], lhsT=pavg[:], rhs=wt["sq"][:],
                        start=True, stop=True),
                        reads=(nm("sq"), nm("pavg"), nm("psC")), writes=(nm("psCv2"),))
                    pb.op("scalar", lambda h: h.activation(
                        out=wt["std2"][:], in_=psC[:, 12:16], func=AF.Sqrt,
                        bias=epsv[:]),
                        reads=(nm("psCv2"), epstok), writes=(nm("std2"), nm("psC")))
                    pb.op("vector", lambda h: h.reciprocal(out=wt["istd2"][:],
                                                           in_=wt["std2"][:]),
                          reads=(nm("std2"),), writes=(nm("istd2"),))
                    pb.op("vector", lambda h: h.tensor_mul(
                        out=wt["xh2"][:], in0=wt["cen2"][:], in1=wt["istd2"][:]),
                        reads=(nm("cen2"), nm("istd2")), writes=(nm("xh2"),))
                    pb.op("vector", lambda h: h.tensor_scalar(
                        out=wt["xh2"][:], in0=wt["xh2"][:], scalar1=gb[:, 0:1],
                        scalar2=None, op0=ALU.mult),
                        reads=(nm("xh2"), nm("gb")), writes=(nm("xh2"),))
                    pb.op("vector", lambda h: h.tensor_scalar(
                        out=wt["xh2"][:], in0=wt["xh2"][:], scalar1=gb[:, 1:2],
                        scalar2=None, op0=ALU.add),
                        reads=(nm("xh2"), nm("gb")), writes=(nm("xh2"),))
                    pb.op("vector", lambda h, qT=qT, col=col: h.tensor_add(
                        out=och[:, col:col + 4], in0=wt["xh2"][:], in1=qT),
                        reads=(nm("xh2"), ctok + "q", ctok + "o"),
                        writes=(ctok + "o",))
                    break
                pb.op("vector", lambda h: h.tensor_scalar(
                    out=wt["blv"][:], in0=wt["vmk"][:], scalar1=p3[:, 2:3],
                    scalar2=-1.0, op0=ALU.subtract, op1=ALU.mult),
                    reads=(nm("vmk"), nm("p3")), writes=(nm("blv"),))
                pb.op("vector", lambda h: h.scalar_tensor_tensor(
                    out=scr[:, 8:12], in0=wt["xhat"][:], scalar=p3[:, 1:2],
                    in1=wt["blv"][:], op0=ALU.mult, op1=ALU.add),
                    reads=(nm("xhat"), nm("p3"), nm("blv")), writes=(nm("scr_r"),))
                pb.op("vector", lambda h: h.tensor_scalar(
                    out=wt["t1"][:], in0=scr[:, 8:12], scalar1=p3[:, 1:2],
                    scalar2=None, op0=ALU.mult),
                    reads=(nm("scr_r"), nm("p3")), writes=(nm("t1"),))
                pb.op("tensor", lambda h: h.matmul(
                    out=psC[:, 8:12], lhsT=pavg[:], rhs=wt["t1"][:],
                    start=True, stop=True),
                    reads=(nm("t1"), nm("pavg"), nm("psC")), writes=(nm("psCm1"),))
                pb.op("vector", lambda h: h.tensor_mul(
                    out=wt["t2"][:], in0=wt["t1"][:], in1=wt["xhat"][:]),
                    reads=(nm("t1"), nm("xhat")), writes=(nm("t2"),))
                pb.op("tensor", lambda h: h.matmul(
                    out=psC[:, 12:16], lhsT=pavg[:], rhs=wt["t2"][:],
                    start=True, stop=True),
                    reads=(nm("t2"), nm("pavg"), nm("psC")), writes=(nm("psCm2"),))
                pb.op("vector", lambda h: h.tensor_mul(
                    out=wt["u"][:], in0=wt["xhat"][:], in1=psC[:, 12:16]),
                    reads=(nm("xhat"), nm("psCm2")), writes=(nm("u"), nm("psC")))
                pb.op("vector", lambda h: h.tensor_sub(
                    out=wt["w1"][:], in0=wt["t1"][:], in1=psC[:, 8:12]),
                    reads=(nm("t1"), nm("psCm1")), writes=(nm("w1"), nm("psC")))
                pb.op("vector", lambda h: h.tensor_sub(
                    out=wt["w2b"][:], in0=wt["w1"][:], in1=wt["u"][:]),
                    reads=(nm("w1"), nm("u")), writes=(nm("w2b"),))
                pb.op("vector", lambda h: h.tensor_mul(
                    out=scr[:, 0:4], in0=wt["w2b"][:], in1=wt["istd"][:]),
                    reads=(nm("w2b"), nm("istd")), writes=(nm("scr_d"),))
                pb.op("vector", lambda h: h.tensor_mul(
                    out=scr[:, 4:8], in0=scr[:, 8:12], in1=wt["xhat"][:]),
                    reads=(nm("scr_r"), nm("xhat")), writes=(nm("scr_rx"),))
                pb.op("tensor", lambda h: h.transpose(
                    out=psE[:, 128:256], in_=scr[:, 0:4], identity=identf[:]),
                    reads=(nm("scr_d"), "identf", nm("psE")), writes=(nm("psEd"),))
                pb.op("scalar", lambda h: h.copy(out=dtm[:], in_=psE[:, 128:256]),
                      reads=(nm("psEd"),), writes=(nm("dtm"), nm("psE")))
                pb.op("tensor", lambda h: h.matmul(
                    out=psD[0:64, 0:64], lhsT=ktm[0:4, 0:64], rhs=dtm[0:4, 0:64],
                    start=True, stop=True),
                    reads=(nm("ktm"), nm("dtm"), nm("psD")), writes=(nm("psDw0"),))
                pb.op("tensor", lambda h: h.matmul(
                    out=psD[64:128, 64:128], lhsT=ktm[0:4, 64:128], rhs=dtm[0:4, 64:128],
                    start=True, stop=True, skip_group_check=True),
                    reads=(nm("ktm"), nm("dtm"), nm("psD")), writes=(nm("psDw1"),))
                pb.op("vector", lambda h: h.scalar_tensor_tensor(
                    out=W2[0:64, 0:64], in0=psD[0:64, 0:64], scalar=-LAM,
                    in1=W2[0:64, 0:64], op0=ALU.mult, op1=ALU.add),
                    reads=(nm("psDw0"), nm("W2")), writes=(nm("W2"),))
                pb.op("vector", lambda h: h.scalar_tensor_tensor(
                    out=W2[64:128, 64:128], in0=psD[64:128, 64:128], scalar=-LAM,
                    in1=W2[64:128, 64:128], op0=ALU.mult, op1=ALU.add),
                    reads=(nm("psDw1"), nm("W2")), writes=(nm("W2"), nm("psD")))
                pb.op("vector", lambda h: h.tensor_reduce(
                    out=red3[:], in_=scr.rearrange("p (k b) -> p k b", k=3),
                    axis=AX.X, op=ALU.add),
                    reads=(nm("scr_d"), nm("scr_r"), nm("scr_rx")),
                    writes=(nm("red3"),))
                pb.op("vector", lambda h: h.scalar_tensor_tensor(
                    out=p3[:], in0=red3.rearrange("p (a k) -> p a k", a=1)[:, 0, :].rearrange("p (a k) -> p a k", a=1)[:, 0, :] if False else red3[:],
                    scalar=-LAM, in1=p3[:], op0=ALU.mult, op1=ALU.add),
                    reads=(nm("red3"), nm("p3")), writes=(nm("p3"),))
        pb.dma(out_dram[:, ci * CW:(ci + 1) * CW], och[:, cb * CW:(cb + 1) * CW],
               reads=(ctok + "o",), writes=(ctok + "o", f"{name}_outd"),
               stream=f"st_{name}_o{cb}")
    soW = pb.sbuf(f"{name}_soW", [P, 64], F32)
    pb.op("vector", lambda h: h.tensor_copy(out=soW[0:64, :], in_=W2[0:64, 0:64]),
          reads=(nm("W2"),), writes=(nm("soW"),))
    pb.op("vector", lambda h: h.tensor_copy(out=soW[64:128, :], in_=W2[64:128, 64:128]),
          reads=(nm("W2"),), writes=(nm("soW"),))
    pb.dma(pb.dram("so_W", [P, 64], F32, kind="ExternalOutput"), soW[:],
           reads=(nm("soW"),), writes=(nm("soWst"),))
    pb.dma(pb.dram("so_p3", [P, 3], F32, kind="ExternalOutput"), p3[:],
           reads=(nm("p3"),), writes=(nm("sop3st"),))
    pb.wait_all("sync", [f"{name}_outd", nm("soWst"), nm("sop3st")])


def emit_ttt_post(pb, x_dram, xout_dram, T, name, outT_dram, gate_dram, ident):
    identb = pb._mem["ident_sb"]
    ones1 = pb._mem["ones1_sb"]
    wo, _ = emit_load_wT(pb, f"{name}_wo", pb.dram("L1_attn_a_Wo_T", [D, D], F32,
                                                   kind="ExternalInput"), D, D)
    wor = wo.rearrange("p (c n) -> p c n", c=NCH)
    pgrep, _ = emit_bcast_row(pb, f"{name}_pg", pb.dram("L1_attn_a_post_g", [D], F32,
                                                        kind="ExternalInput"), D, ones1)
    pbrep, _ = emit_bcast_row(pb, f"{name}_pb", pb.dram("L1_attn_a_post_b", [D], F32,
                                                        kind="ExternalInput"), D, ones1)
    otT = pb.sbuf(f"{name}_otT", [P, D], BF16)
    gtt = pb.sbuf(f"{name}_gtt", [P, D], BF16)
    otok = pb.sbuf(f"{name}_otok", [P, D], F32)
    ytok = pb.sbuf(f"{name}_ytok", [P, D], F32)
    ygT = pb.sbuf(f"{name}_ygT", [P, D], BF16)
    xres = pb.sbuf(f"{name}_xres", [P, 2 * D], F32)
    xo = pb.sbuf(f"{name}_xo", [P, 2 * D], F32)
    psT1 = pb.psum(f"{name}_psT1", [P, D], BF16)
    psp = [pb.psum(f"{name}_psp{j}", [P, 512], F32) for j in range(2)]
    ntile = T // P
    for i in range(ntile):
        pbuf = i % 2
        pb.dma(otT[:], outT_dram[:, i * P:(i + 1) * P].rearrange("(c p) t -> p c t", p=P),
               reads=(f"{name}_otT",), writes=(f"{name}_otT",), stream=f"st_{name}_ot")
        pb.dma(gtt[:], gate_dram[i * P:(i + 1) * P, :],
               reads=(f"{name}_gtt",), writes=(f"{name}_gtt",), stream=f"st_{name}_gt")
        tok_xr = f"{name}_xr{pbuf}"
        pb.dma(xres[:, pbuf * D:(pbuf + 1) * D], x_dram[i * P:(i + 1) * P, :],
               reads=(tok_xr,), writes=(tok_xr,))
        for c in range(NCH):
            pb.op("tensor", lambda h, c=c: h.transpose(
                out=psT1[:, c * P:(c + 1) * P], in_=otT[:, c * P:(c + 1) * P],
                identity=identb[:]),
                reads=(f"{name}_otT", "ident", f"{name}_psT1"), writes=(f"{name}_psT1w{c}",))
        pb.op("vector", lambda h: h.tensor_copy(out=otok[:], in_=psT1[:]),
              reads=tuple(f"{name}_psT1w{c}" for c in range(NCH)),
              writes=(f"{name}_psT1", f"{name}_otok"))
        emit_ln_tile(pb, f"{name}lt", otok[:], f"{name}_otok", ytok[:], f"{name}_ytok")
        pb.op("vector", lambda h: h.tensor_mul(out=ytok[:], in0=ytok[:], in1=pgrep[:]),
              reads=(f"{name}_ytok", f"{name}_pg"), writes=(f"{name}_ytok",))
        pb.op("vector", lambda h: h.tensor_add(out=ytok[:], in0=ytok[:], in1=pbrep[:]),
              reads=(f"{name}_ytok", f"{name}_pb"), writes=(f"{name}_ytok",))
        ytb = pb.sbuf(f"{name}_ytb", [P, D], BF16)
        pb.op("vector", lambda h: h.tensor_tensor(out=ytb[:], in0=ytok[:], in1=gtt[:],
                                                  op=ALU.mult),
              reads=(f"{name}_ytok", f"{name}_gtt", f"{name}_ytb"), writes=(f"{name}_ytb",))
        for c in range(NCH):
            pb.op("tensor", lambda h, c=c: h.transpose(
                out=psT1[:, c * P:(c + 1) * P], in_=ytb[:, c * P:(c + 1) * P],
                identity=identb[:]),
                reads=(f"{name}_ytb", "ident", f"{name}_psT1"),
                writes=(f"{name}_psT2w{c}",))
        pb.op("vector", lambda h: h.tensor_copy(out=ygT[:], in_=psT1[:]),
              reads=tuple(f"{name}_psT2w{c}" for c in range(NCH)),
              writes=(f"{name}_psT1", f"{name}_ygT"))
        ygr = ygT.rearrange("p (c t) -> p c t", c=NCH)
        for bk in range(2):
            for kc in range(NCH):
                pb.op("tensor", lambda h, kc=kc, bk=bk, ygr=ygr: h.matmul(
                    out=psp[bk][:], lhsT=ygr[:, kc, :], rhs=wor[:, kc, bk * 512:(bk + 1) * 512],
                    start=(kc == 0), stop=(kc == NCH - 1)),
                    reads=(f"{name}_ygT", f"{name}_wo", f"{name}_psp{bk}"),
                    writes=(f"{name}_psp{bk}w",) if kc == NCH - 1 else ())
            tok_xo = f"{name}_xo{pbuf}_{bk}"
            pb.op("vector", lambda h, bk=bk, pbuf=pbuf: h.tensor_add(
                out=xo[:, pbuf * D + bk * 512:pbuf * D + (bk + 1) * 512],
                in0=psp[bk][:], in1=xres[:, pbuf * D + bk * 512:pbuf * D + (bk + 1) * 512]),
                reads=(f"{name}_psp{bk}w", tok_xr),
                writes=(f"{name}_psp{bk}", tok_xo, f"{name}_xo{pbuf}"))
        pb.dma(xout_dram[i * P:(i + 1) * P, :], xo[:, pbuf * D:(pbuf + 1) * D],
               reads=(f"{name}_xo{pbuf}_0", f"{name}_xo{pbuf}_1"),
               writes=(f"{name}_stout{pbuf}",))
    pb.wait_all("sync", [f"{name}_stout0", f"{name}_stout1"])


def build_ttt_proj_launch(T):
    nc = bass.Bass("TRN2", target_bir_lowering=False, debug=False)
    x = nc.dram_tensor("x", [T, D], F32, kind="ExternalInput").ap()
    qkv = nc.dram_tensor("qkv", [3, D, T], BF16, kind="ExternalOutput").ap()
    gate = nc.dram_tensor("gate", [T, D], BF16, kind="ExternalOutput").ap()
    pb = PB(nc)
    ident = emit_const_ident(pb)
    emit_ttt_proj(pb, x, T, "tp", qkv, gate, ident)
    pb.finalize()
    return nc


def build_ttt_post_launch(T):
    nc = bass.Bass("TRN2", target_bir_lowering=False, debug=False)
    x = nc.dram_tensor("x", [T, D], F32, kind="ExternalInput").ap()
    outT = nc.dram_tensor("outT", [D, T], BF16, kind="ExternalInput").ap()
    gate = nc.dram_tensor("gate", [T, D], BF16, kind="ExternalInput").ap()
    xout = nc.dram_tensor("xout", [T, D], F32, kind="ExternalOutput").ap()
    pb = PB(nc)
    ident = emit_const_ident(pb)
    ones1 = pb.sbuf("ones1_sb", [1, P], BF16)
    pb.dma(ones1[:], pb.dram("ones1", [1, P], BF16, kind="ExternalInput"),
           writes=("ones1",))
    emit_ttt_post(pb, x, xout, T, "tq", outT, gate, ident)
    pb.finalize()
    return nc


def build_ttt_scan_launch(nsteps, chunk=1024):
    nc = bass.Bass("TRN2", target_bir_lowering=False, debug=False)
    qs = nc.dram_tensor("qs", [P, nsteps * 4], BF16, kind="ExternalInput").ap()
    ks = nc.dram_tensor("ks", [P, nsteps * 4], BF16, kind="ExternalInput").ap()
    vs = nc.dram_tensor("vs", [P, nsteps * 4], BF16, kind="ExternalInput").ap()
    outd = nc.dram_tensor("outs", [P, nsteps * 4], BF16, kind="ExternalOutput").ap()
    pb = PB(nc)
    identf = pb.sbuf("identf_sb", [P, P], F32)
    pb.dma(identf[:], pb.dram("identf", [P, P], F32, kind="ExternalInput"),
           writes=("identf",))
    emit_ttt_scan(pb, "tt", nsteps, qs, ks, vs, outd, chunk=chunk)
    pb.finalize()
    return nc


def host_ttt_consts(inp, core):
    h0, h1 = 2 * core, 2 * core + 1
    W = np.concatenate([inp["L1_attn_a_fw_W"][h0], inp["L1_attn_a_fw_W"][h1]], 0).astype(np.float32)
    p3 = np.stack([
        np.concatenate([inp["L1_attn_a_fw_b"][h0], inp["L1_attn_a_fw_b"][h1]]),
        np.concatenate([inp["L1_attn_a_fw_g"][h0], inp["L1_attn_a_fw_g"][h1]]),
        np.concatenate([inp["L1_attn_a_fw_bl"][h0], inp["L1_attn_a_fw_bl"][h1]]),
    ], 1).astype(np.float32)
    gbv = np.stack([
        np.concatenate([inp["L1_attn_a_ttt_g"][h0], inp["L1_attn_a_ttt_g"][h1]]),
        np.concatenate([inp["L1_attn_a_ttt_b"][h0], inp["L1_attn_a_ttt_b"][h1]]),
    ], 1).astype(np.float32)
    pavg = np.zeros((P, P), np.float32)
    pavg[0:64, 0:64] = 1.0 / 64
    pavg[64:128, 64:128] = 1.0 / 64
    return {"c_ttt_W": W, "c_ttt_p3": p3, "c_ttt_gb": gbv, "c_pavg": pavg,
            "identf": np.eye(P, dtype=np.float32)}



def host_dft_consts():
    n = np.arange(D)
    ang = 2 * np.pi * np.outer(n, n) / D
    fr = np.cos(ang).astype(np.float32)
    fi = (-np.sin(ang)).astype(np.float32)
    gr = (np.cos(ang) / D).astype(np.float32)
    gni = (-np.sin(ang) / D).astype(np.float32)
    return fr, fi, gr, gni


def host_ident():
    return np.eye(P, dtype=BF)



def build_attn_a_launch(T):
    nc = bass.Bass("TRN2", target_bir_lowering=False, debug=False)
    x = nc.dram_tensor("x", [T, D], F32, kind="ExternalInput").ap()
    xout = nc.dram_tensor("xout", [T, D], F32, kind="ExternalOutput").ap()
    pb = PB(nc)
    emit_const_ident(pb)
    emit_attn_a(pb, x, xout, T)
    pb.finalize()
    return nc


def attn_a_inputs(x_shard, inp):
    cs, sn, mask = host_attn_a_consts()
    return {
        "x": x_shard,
        "ident": host_ident(),
        "c_ropec": cs, "c_ropes": sn, "c_chmask": mask,
        "L0_attn_a_w_attn_T": np.ascontiguousarray(inp["L0_attn_a_w_attn"].T),
        "L0_attn_a_w_proj_T": np.ascontiguousarray(inp["L0_attn_a_w_proj"].T),
    }


def build_moe_mask_launch(T):
    nc = bass.Bass("TRN2", target_bir_lowering=False, debug=False)
    x = nc.dram_tensor("x", [T, D], F32, kind="ExternalInput").ap()
    partial = nc.dram_tensor("gpart", [1, 16], F32, kind="ExternalOutput").ap()
    maskd = nc.dram_tensor("maskout", [T, 8], F32, kind="ExternalOutput").ap()
    pb = PB(nc)
    ident = emit_const_ident(pb)
    ones1 = pb.sbuf("ones1_sb", [1, P], BF16)
    pb.dma(ones1[:], pb.dram("ones1", [1, P], BF16, kind="ExternalInput"),
           writes=("ones1",))
    emit_moe_mask(pb, x, T, "mm", "lnmm", partial, maskd, ident)
    pb.finalize()
    return nc


def build_moe_expert_launch(T):
    nc = bass.Bass("TRN2", target_bir_lowering=False, debug=False)
    x = nc.dram_tensor("x", [T, D], F32, kind="ExternalInput").ap()
    maskd = nc.dram_tensor("maskin", [T, 8], F32, kind="ExternalInput").ap()
    gtrd = nc.dram_tensor("gtr", [8, 16], F32, kind="ExternalInput").ap()
    xout = nc.dram_tensor("xout", [T, D], F32, kind="ExternalOutput").ap()
    pb = PB(nc)
    ident = emit_const_ident(pb)
    ones1 = pb.sbuf("ones1_sb", [1, P], BF16)
    pb.dma(ones1[:], pb.dram("ones1", [1, P], BF16, kind="ExternalInput"),
           writes=("ones1",))
    emit_moe_experts(pb, x, xout, T, "me", maskd, gtrd, ident)
    pb.finalize()
    return nc


def build_l1ffn_launch(T):
    nc = bass.Bass("TRN2", target_bir_lowering=False, debug=False)
    x = nc.dram_tensor("x", [T, D], F32, kind="ExternalInput").ap()
    xout = nc.dram_tensor("xout", [T, D], F32, kind="ExternalOutput").ap()
    pb = PB(nc)
    ident = emit_const_ident(pb)
    ones1 = pb.sbuf("ones1_sb", [1, P], BF16)
    pb.dma(ones1[:], pb.dram("ones1", [1, P], BF16, kind="ExternalInput"),
           writes=("ones1",))
    mr = emit_l1ffn_mr(pb)
    bias_dram = pb.dram("L1_ffn_bias", [D], F32, kind="ExternalInput")
    biasrep, _ = emit_bcast_row(pb, "biasrep", bias_dram, D, ones1)
    emit_l1ffn(pb, x, xout, T, mr, biasrep)
    pb.finalize()
    return nc


def l1ffn_inputs(x_shard, inp):
    fr, fi, gr, gni = host_dft_consts()
    return {
        "x": x_shard,
        "ident": host_ident(),
        "ones1": np.ones((1, P), dtype=BF),
        "c_fr": fr, "c_fi": fi, "c_gr": gr, "c_gni": gni,
        "L1_ffn_w_freq_real": inp["L1_ffn_w_freq_real"],
        "L1_ffn_w_freq_imag": inp["L1_ffn_w_freq_imag"],
        "L1_ffn_bias": inp["L1_ffn_bias"],
    }



TCORE = 4096
TTT_SPLIT = 4


def _shard(x):
    return [np.ascontiguousarray(x[c // 2, (c % 2) * TCORE:(c % 2 + 1) * TCORE, :])
            for c in range(NCORES)]


def _unshard(parts):
    x = np.empty((4, 8192, D), np.float32)
    for c in range(NCORES):
        x[c // 2, (c % 2) * TCORE:(c % 2 + 1) * TCORE, :] = parts[c]
    return x


def kernel(**inputs):
    inp = inputs
    T = TCORE
    xs = _shard(np.asarray(inp["L0_attn_a_x"], np.float32))
    ident = host_ident()
    ones1 = np.ones((1, P), dtype=BF)
    identf = np.eye(P, dtype=np.float32)

    nc = build_attn_a_launch(T)
    res = run_bass_kernel_spmd(nc, [attn_a_inputs(xs[c], inp) for c in range(NCORES)],
                               core_ids=list(range(NCORES)))
    xs = [res.results[c]["xout"] for c in range(NCORES)]

    def run_moe(xs):
        common = {"ident": ident, "ones1": ones1}
        ncA = build_moe_mask_launch(T)
        imsA = [{"x": xs[c], "L0_ffn_router_w": inp["L0_ffn_router_w"],
                 "L0_ffn_router_b": inp["L0_ffn_router_b"], **common}
                for c in range(NCORES)]
        resA = run_bass_kernel_spmd(ncA, imsA, core_ids=list(range(NCORES)))
        gtr = np.zeros((8, 16), np.float32)
        for c in range(NCORES):
            part = np.asarray(resA.results[c]["gpart"][0], np.float32)
            for e in range(8):
                for j in range(2):
                    gtr[e, j * 8 + c] = part[j * 8 + e]
        ncB = build_moe_expert_launch(T)
        imsB = []
        for c in range(NCORES):
            flag = np.full((P, 1), 1.0 if c == 0 else 0.0, np.float32)
            imsB.append({"x": xs[c], "maskin": resA.results[c]["maskout"], "gtr": gtr,
                         "c_core0": flag, "L0_ffn_w1": inp["L0_ffn_w1"],
                         "L0_ffn_b1": inp["L0_ffn_b1"], "L0_ffn_w2": inp["L0_ffn_w2"],
                         "L0_ffn_b2": inp["L0_ffn_b2"], **common})
        resB = run_bass_kernel_spmd(ncB, imsB, core_ids=list(range(NCORES)))
        return [resB.results[c]["xout"] for c in range(NCORES)]

    xs = run_moe(xs)

    common_ab = {"ident": ident, "ones1": ones1, "c_triu": host_triu(),
                 "L0_attn_b_Wq_T": np.ascontiguousarray(inp["L0_attn_b_Wq"].T),
                 "L0_attn_b_Wk_T": np.ascontiguousarray(inp["L0_attn_b_Wk"].T),
                 "L0_attn_b_Wv_T": np.ascontiguousarray(inp["L0_attn_b_Wv"].T),
                 "L0_attn_b_Wo_T": np.ascontiguousarray(inp["L0_attn_b_Wo"].T),
                 "L0_attn_b_bq": inp["L0_attn_b_bq"], "L0_attn_b_bk": inp["L0_attn_b_bk"],
                 "L0_attn_b_bv": inp["L0_attn_b_bv"], "L0_attn_b_bo": inp["L0_attn_b_bo"]}
    ncS = build_attn_b_scan_launch(T)
    resS = run_bass_kernel_spmd(ncS, [{"x": xs[c], **common_ab} for c in range(NCORES)],
                                core_ids=list(range(NCORES)))
    ncO = build_attn_b_out_launch(T)
    imsO = []
    for c in range(NCORES):
        carry = (np.asarray(resS.results[c - 1]["states"], np.float32)
                 if c % 2 == 1 else np.zeros((16, 64, 65), np.float32))
        imsO.append({"x": xs[c], "numden": resS.results[c]["numden"],
                     "carry": carry, **common_ab})
    resO = run_bass_kernel_spmd(ncO, imsO, core_ids=list(range(NCORES)))
    xs = [resO.results[c]["xout"] for c in range(NCORES)]

    xs = run_moe(xs)

    common_tt = {"ident": ident, "ones1": ones1,
                 "L1_attn_a_Wq_T": np.ascontiguousarray(inp["L1_attn_a_Wq"].T),
                 "L1_attn_a_Wk_T": np.ascontiguousarray(inp["L1_attn_a_Wk"].T),
                 "L1_attn_a_Wv_T": np.ascontiguousarray(inp["L1_attn_a_Wv"].T),
                 "L1_attn_a_Wg_T": np.ascontiguousarray(inp["L1_attn_a_Wg"].T),
                 "L1_attn_a_Wo_T": np.ascontiguousarray(inp["L1_attn_a_Wo"].T),
                 "L1_attn_a_post_g": inp["L1_attn_a_post_g"],
                 "L1_attn_a_post_b": inp["L1_attn_a_post_b"]}
    ncP = build_ttt_proj_launch(T)
    resP = run_bass_kernel_spmd(ncP, [{"x": xs[c], **common_tt} for c in range(NCORES)],
                                core_ids=list(range(NCORES)))
    qkvs = [np.asarray(resP.results[c]["qkv"]) for c in range(NCORES)]
    gates = [np.asarray(resP.results[c]["gate"]) for c in range(NCORES)]
    scan_in = []
    for c2 in range(NCORES):
        arrs = []
        for slot in range(3):
            a = np.empty((P, 8192, 4), dtype=BF)
            for c in range(NCORES):
                b, th = c // 2, c % 2
                a[:, th * TCORE:(th + 1) * TCORE, b] = qkvs[c][slot, c2 * P:(c2 + 1) * P, :]
            arrs.append(a.reshape(P, 8192 * 4))
        scan_in.append(arrs)
    NSPL = 8192 // TTT_SPLIT
    ncT = build_ttt_scan_launch(NSPL, chunk=1024)
    states = [host_ttt_consts(inp, c2) for c2 in range(NCORES)]
    outs_parts = [[] for _ in range(NCORES)]
    for sp in range(TTT_SPLIT):
        ims = []
        for c2 in range(NCORES):
            st = states[c2]
            ims.append({"qs": scan_in[c2][0][:, sp * NSPL * 4:(sp + 1) * NSPL * 4],
                        "ks": scan_in[c2][1][:, sp * NSPL * 4:(sp + 1) * NSPL * 4],
                        "vs": scan_in[c2][2][:, sp * NSPL * 4:(sp + 1) * NSPL * 4],
                        **st})
        resT = run_bass_kernel_spmd(ncT, ims, core_ids=list(range(NCORES)))
        for c2 in range(NCORES):
            outs_parts[c2].append(np.asarray(resT.results[c2]["outs"]))
            states[c2] = dict(states[c2])
            states[c2]["c_ttt_W"] = np.asarray(resT.results[c2]["so_W"], np.float32)
            states[c2]["c_ttt_p3"] = np.asarray(resT.results[c2]["so_p3"], np.float32)
    outs_full = [np.concatenate(outs_parts[c2], axis=1) for c2 in range(NCORES)]
    ncQ = build_ttt_post_launch(T)
    imsQ = []
    for c in range(NCORES):
        b, th = c // 2, c % 2
        oT = np.empty((D, TCORE), dtype=BF)
        for c2 in range(NCORES):
            o = outs_full[c2].reshape(P, 8192, 4)
            oT[c2 * P:(c2 + 1) * P, :] = o[:, th * TCORE:(th + 1) * TCORE, b]
        imsQ.append({"x": xs[c], "outT": oT, "gate": gates[c], **common_tt})
    resQ = run_bass_kernel_spmd(ncQ, imsQ, core_ids=list(range(NCORES)))
    xs = [resQ.results[c]["xout"] for c in range(NCORES)]

    def run_sffn(xs):
        ncF = build_l1ffn_launch(T)
        ims = [l1ffn_inputs(xs[c], inp) for c in range(NCORES)]
        resF = run_bass_kernel_spmd(ncF, ims, core_ids=list(range(NCORES)))
        return [resF.results[c]["xout"] for c in range(NCORES)]

    xs = run_sffn(xs)

    common_rw = {"ident": ident,
                 "L1_attn_b_w_k_T": np.ascontiguousarray(inp["L1_attn_b_w_k"].T),
                 "L1_attn_b_w_v_T": np.ascontiguousarray(inp["L1_attn_b_w_v"].T),
                 "L1_attn_b_w_r_T": np.ascontiguousarray(inp["L1_attn_b_w_r"].T),
                 "L1_attn_b_w_o_T": np.ascontiguousarray(inp["L1_attn_b_w_o"].T),
                 "L1_attn_b_time_decay": inp["L1_attn_b_time_decay"],
                 "L1_attn_b_time_first": inp["L1_attn_b_time_first"],
                 "L1_attn_b_time_mix_key": inp["L1_attn_b_time_mix_key"],
                 "L1_attn_b_time_mix_value": inp["L1_attn_b_time_mix_value"],
                 "L1_attn_b_time_mix_receptance": inp["L1_attn_b_time_mix_receptance"]}
    zrow = np.zeros((1, D), np.float32)
    ncR = build_rwkv_launch(T, "scan")
    imsR = []
    for c in range(NCORES):
        xprev = zrow if c % 2 == 0 else np.ascontiguousarray(xs[c - 1][T - 1:T, :])
        imsR.append({"x": xs[c], "c_xprev": xprev, **common_rw})
    resR = run_bass_kernel_spmd(ncR, imsR, core_ids=list(range(NCORES)))
    ncR2 = build_rwkv_launch(T, "out")
    imsR2 = []
    for c in range(NCORES):
        xprev = zrow if c % 2 == 0 else np.ascontiguousarray(xs[c - 1][T - 1:T, :])
        carry = (np.asarray(resR.results[c - 1]["states"], np.float32)
                 if c % 2 == 1 else np.zeros((D, 2), np.float32))
        imsR2.append({"x": xs[c], "c_xprev": xprev, "carry": carry, **common_rw})
    resR2 = run_bass_kernel_spmd(ncR2, imsR2, core_ids=list(range(NCORES)))
    xs = [resR2.results[c]["xout"] for c in range(NCORES)]

    xs = run_sffn(xs)
    return _unshard([np.asarray(x, np.float32) for x in xs])
